# version 05
# baseline (speedup 1.0000x reference)
import contextlib
import numpy as np
import ml_dtypes
import concourse.bass as bass
import concourse.mybir as mybir
from concourse.bass_utils import run_bass_kernel_spmd

F32 = mybir.dt.float32
BF16 = mybir.dt.bfloat16
AF = mybir.ActivationFunctionType
ALU = mybir.AluOpType
BT = ml_dtypes.bfloat16

D = 1024
HID = 2048
TCORE = 2048
NCORE = 8


class K:

    def __init__(self, nc, es, es_sem=None):
        self.nc = nc
        self.es = es
        self.es_sem = es_sem if es_sem is not None else es
        self.sems = {}

    def s(self, name):
        if name not in self.sems:
            h = self.es_sem.enter_context(self.nc.semaphore(name))
            self.sems[name] = [h, 0]
        return self.sems[name]

    def snapshot(self):
        return {n: e[1] for n, e in self.sems.items()}

    def changed_since(self, snap):
        return [(n, e[1]) for n, e in self.sems.items()
                if e[1] != snap.get(n, 0)]

    def inc(self, instr, name, k=1):
        ent = self.s(name)
        instr.then_inc(ent[0], k)
        ent[1] += k
        return ent[1]

    def tok(self, name):
        return self.s(name)[1]

    def wait(self, eng, name, val=None):
        ent = self.s(name)
        v = ent[1] if val is None else val
        if v > 0:
            eng.wait_ge(ent[0], v)

    def chain(self, eng, instr, name):
        self.inc(instr, name, 1)
        self.wait(eng, name)



def emit_weight_load(k, blk, src_f32, dsts, stg, piece, qname, cname,
                     prev=None):
    R, C = src_f32.shape
    nb = R // 128
    jobs = []
    for b in range(nb):
        for c0 in range(0, C, piece):
            jobs.append((b, c0, min(piece, C - c0)))
    sname, pname = cname + "s", cname + "p"
    q0 = k.tok(qname)
    s0, p0 = k.tok(sname), k.tok(pname)

    @blk.sync
    def _(sync):
        if prev is not None:
            k.wait(sync, prev[0][0], prev[1][0])
            k.wait(sync, prev[0][1], prev[1][1])
        for j, (b, c0, w) in enumerate(jobs):
            if j >= 2:
                jj = j - 2
                k.wait(sync, sname if jj % 2 == 0 else pname,
                       (s0 if jj % 2 == 0 else p0) + jj // 2 + 1)
            k.inc(sync.dma_start(out=stg[j % 2][:, 0:w],
                                 in_=src_f32[b * 128:(b + 1) * 128, c0:c0 + w]),
                  qname, 16)

    s_base, p_base = k.tok(sname), k.tok(pname)

    @blk.scalar
    def _(scalar):
        for j, (b, c0, w) in enumerate(jobs):
            if j % 2 != 0:
                continue
            k.wait(scalar, qname, q0 + 16 * (j + 1))
            k.inc(scalar.copy(out=dsts[b][:, c0:c0 + w], in_=stg[j % 2][:, 0:w]),
                  sname, 1)

    @blk.gpsimd
    def _(gpsimd):
        for j, (b, c0, w) in enumerate(jobs):
            if j % 2 != 1:
                continue
            k.wait(gpsimd, qname, q0 + 16 * (j + 1))
            k.inc(gpsimd.tensor_copy(out=dsts[b][:, c0:c0 + w],
                                     in_=stg[j % 2][:, 0:w]), pname, 1)

    return ([sname, pname], [k.tok(sname), k.tok(pname)])



def emit_norm(k, blk, *, W, npb, onesb, onesrow, xb16, sqscr, rows, bc_ps,
              su_ps, sq_ps, out, scr, eps, with_mean, pre, tag,
              xf32=None, bc_war=0, stat_war=0, out_war=()):
    sname, sval = pre
    S, T, V, VN = "s" + tag, "t" + tag, "v" + tag, "v" + tag + "n"

    @blk.scalar
    def _(scalar):
        k.wait(scalar, sname, sval)
        k.wait(scalar, T, k.tok(T))
        k.inc(scalar.activation(out=sqscr[:, 0:npb * W], in_=xb16[:, 0:npb * W],
                                func=AF.Square), S, 1)

    @blk.tensor
    def _(tensor):
        k.wait(tensor, S, k.tok(S))
        if stat_war:
            k.wait(tensor, V, stat_war)
        if with_mean:
            for pb in range(npb):
                tensor.matmul(out=su_ps[:, 0:W], lhsT=onesb[:, 0:1],
                              rhs=xb16[:, pb * W:(pb + 1) * W],
                              start=(pb == 0), stop=(pb == npb - 1))
        for pb in range(npb):
            ins = tensor.matmul(out=sq_ps[:, 0:W], lhsT=onesb[:, 0:1],
                                rhs=sqscr[:, pb * W:(pb + 1) * W],
                                start=(pb == 0), stop=(pb == npb - 1))
        k.inc(ins, T, 1)

    inv = 1.0 / (npb * 128)

    if with_mean:
        @blk.vector
        def _(vector):
            k.wait(vector, T, k.tok(T))
            k.inc(vector.tensor_scalar_mul(out=rows[:, 0:W], in0=su_ps[:, 0:W],
                                           scalar1=inv), V, 1)

        @blk.scalar
        def _(scalar):
            k.wait(scalar, V, k.tok(V))
            k.inc(scalar.activation(out=rows[:, W:2 * W], in_=rows[:, 0:W],
                                    func=AF.Square), S, 1)

        @blk.vector
        def _(vector):
            k.wait(vector, S, k.tok(S))
            k.chain(vector, vector.scalar_tensor_tensor(
                out=rows[:, 4 * W:5 * W], in0=sq_ps[:, 0:W], scalar=inv,
                in1=rows[:, W:2 * W], op0=ALU.mult, op1=ALU.subtract), V)
            k.inc(vector.tensor_scalar(out=rows[:, W:2 * W],
                                       in0=rows[:, 4 * W:5 * W],
                                       scalar1=1.0, scalar2=eps,
                                       op0=ALU.mult, op1=ALU.add), V, 1)
    else:
        @blk.vector
        def _(vector):
            k.wait(vector, T, k.tok(T))
            k.inc(vector.tensor_scalar(out=rows[:, W:2 * W],
                                       in0=sq_ps[:, 0:W],
                                       scalar1=inv, scalar2=eps,
                                       op0=ALU.mult, op1=ALU.add), V, 1)

    @blk.scalar
    def _(scalar):
        k.wait(scalar, V, k.tok(V))
        scalar.activation(out=rows[:, 4 * W:5 * W], in_=rows[:, W:2 * W],
                          func=AF.Ln)
        k.inc(scalar.activation(out=rows[:, 2 * W:3 * W], in_=rows[:, 4 * W:5 * W],
                                func=AF.Exp, scale=-0.5), S, 1)

    if with_mean:
        @blk.vector
        def _(vector):
            k.wait(vector, S, k.tok(S))
            k.inc(vector.tensor_mul(out=rows[:, 3 * W:4 * W], in0=rows[:, 0:W],
                                    in1=rows[:, 2 * W:3 * W]), V, 1)

    @blk.tensor
    def _(tensor):
        k.wait(tensor, S, k.tok(S))
        if with_mean:
            k.wait(tensor, V, k.tok(V))
        if bc_war:
            k.wait(tensor, VN, bc_war)
        ins = tensor.matmul(out=bc_ps[:, 0:W], lhsT=onesrow[:, 0:128],
                            rhs=rows[:, 2 * W:3 * W], start=True, stop=True)
        if with_mean:
            ins = tensor.matmul(out=bc_ps[:, W:2 * W], lhsT=onesrow[:, 0:128],
                                rhs=rows[:, 3 * W:4 * W], start=True, stop=True)
        k.inc(ins, T, 1)

    @blk.vector
    def _(vector):
        k.wait(vector, T, k.tok(T))
        for (snm, sv) in out_war:
            k.wait(vector, snm, sv)
        src = xf32 if xf32 is not None else xb16
        for pb in range(npb):
            if with_mean:
                k.chain(vector, vector.tensor_mul(
                    out=scr[:, 0:W], in0=src[:, pb * W:(pb + 1) * W],
                    in1=bc_ps[:, 0:W]), V)
                ins = vector.tensor_sub(out=out[:, pb * W:(pb + 1) * W],
                                        in0=scr[:, 0:W], in1=bc_ps[:, W:2 * W])
            else:
                ins = vector.tensor_mul(out=out[:, pb * W:(pb + 1) * W],
                                        in0=src[:, pb * W:(pb + 1) * W],
                                        in1=bc_ps[:, 0:W])
        k.inc(ins, VN, 1)



def emit_ph1(k, blk, io):
    import os
    cut1 = os.environ.get("PH1_CUT", "full")
    nc = k.nc
    TC = 512
    NCH = TCORE // TC
    es = k.es

    def sb(name, shape, dt):
        return es.enter_context(nc.sbuf_tensor(name, list(shape), dt))

    def ps(name, shape, dt=F32):
        return es.enter_context(nc.psum_tensor(name, list(shape), dt))

    winT = [sb(f"winT{i}", [128, 4096], BF16) for i in range(8)]
    stg = [sb(f"p1stg{i}", [128, 2048], F32) for i in range(2)]
    x_s = [sb(f"x_s{i}", [128, 8 * TC], F32) for i in range(2)]
    xb_s = sb("xb_s", [128, 8 * TC], BF16)
    sq_s = sb("sq_s", [128, 8 * TC], BF16)
    xn_s = sb("xn_s", [128, 8 * TC], BF16)
    xcpre = sb("xcpre", [128, 16 * (TC + 3)], BF16)
    xc_s = sb("xc_s", [128, 16 * TC], BF16)
    gg_s = sb("gg_s", [128, 16 * TC], BF16)
    rows = sb("p1rows", [1, 5 * TC], F32)
    scr_v = sb("p1scr", [128, TC], F32)
    cscrA = sb("cscrA", [128, TC], F32)
    cscrB = sb("cscrB", [128, TC], F32)
    cscrC = sb("cscrC", [128, TC], F32)
    cscrD = sb("cscrD", [128, TC], F32)
    onesb = sb("onesb1", [128, 1], BF16)
    onesrow = sb("onesrow1", [1, 128], F32)
    cvw = sb("cvw", [128, 64], F32)
    cvb = sb("cvb", [128, 16], F32)
    xh_s = sb("xh_s", [128, 24], F32)
    xhb_s = sb("xhb_s", [128, 24], BF16)
    xhq_s = sb("xhq_s", [128, 24], BF16)
    xnh_s = sb("xnh_s", [128, 24], BF16)
    hrows = sb("hrows", [1, 15], F32)
    hscr = sb("hscr", [128, 3], F32)

    su_ps = ps("su_ps", [1, TC])
    sq_ps = ps("sq_ps", [1, TC])
    bc_ps = ps("bc_ps", [128, 2 * TC])
    gp_ps = [ps(f"gp_ps{i}", [128, TC]) for i in range(2)]

    if cut1 == "c1":
        @blk.gpsimd
        def _(gpsimd):
            k.inc(gpsimd.dma_start(out=io["XN"][0:128, 0:16],
                                   in_=io["convb"]), "qst", 16)
            k.wait(gpsimd, "qst", k.tok("qst"))
        return

    @blk.sync
    def _(sync):
        k.inc(sync.dma_start(out=onesb[:], in_=io["onesb"]), "q", 16)
        k.inc(sync.dma_start(out=onesrow[:], in_=io["onesrow"]), "q", 16)
        k.inc(sync.dma_start(out=cvw[:], in_=io["convw"]), "q", 16)
        k.inc(sync.dma_start(out=cvb[:], in_=io["convb"]), "q", 16)
        k.inc(sync.dma_start(out=xh_s.ap().rearrange("p (pb t) -> p pb t", t=3),
                             in_=io["xhaloT"].rearrange("(pb p) t -> p pb t", p=128)),
              "q", 16)
    consts_tok = k.tok("q")

    if cut1 != "c0":
        wsems, wvals = emit_weight_load(k, blk, io["W_inT"], winT, stg,
                                        piece=2048, qname="q", cname="cw")

    if cut1 == "c0":
        @blk.gpsimd
        def _(gpsimd):
            k.wait(gpsimd, "q", k.tok("q"))
            k.inc(gpsimd.dma_start(out=io["XN"][0:128, 0:16],
                                   in_=cvb[:]), "qst", 16)
            k.wait(gpsimd, "qst", k.tok("qst"))
        return

    if cut1 == "wl":
        @blk.gpsimd
        def _(gpsimd):
            k.wait(gpsimd, "q", k.tok("q"))
            k.wait(gpsimd, wsems[0], wvals[0])
            k.wait(gpsimd, wsems[1], wvals[1])
            k.inc(gpsimd.dma_start(out=io["XN"][0:128, 0:16],
                                   in_=cvb[:]), "qst", 16)
            k.wait(gpsimd, "qst", k.tok("qst"))
        return

    @blk.gpsimd
    def _(gpsimd):
        k.wait(gpsimd, "q", consts_tok)
        k.inc(gpsimd.tensor_copy(out=xhb_s[:], in_=xh_s[:]), "p", 1)

    emit_norm(k, blk, W=3, npb=8, onesb=onesb, onesrow=onesrow,
              xb16=xhb_s, sqscr=xhq_s, rows=hrows, bc_ps=bc_ps,
              su_ps=su_ps, sq_ps=sq_ps, out=xnh_s, xf32=xh_s,
              scr=hscr, eps=1e-5, with_mean=True,
              pre=("p", k.tok("p")), tag="")

    s1_halo = k.tok("s")
    t1_halo = k.tok("t")

    @blk.tensor
    def _(tensor):
        k.wait(tensor, wsems[0], wvals[0])
        k.wait(tensor, wsems[1], wvals[1])
        k.wait(tensor, "vn", k.tok("vn"))
        for i, m in enumerate(range(16, 32)):
            if i >= 1:
                k.wait(tensor, "s", s1_halo + i)
            for kb in range(8):
                ins = tensor.matmul(out=gp_ps[0][:, 0:3],
                                    lhsT=winT[kb][:, m * 128:(m + 1) * 128],
                                    rhs=xnh_s[:, kb * 3:(kb + 1) * 3],
                                    start=(kb == 0), stop=(kb == 7))
            k.inc(ins, "t", 1)

    @blk.scalar
    def _(scalar):
        for i, m in enumerate(range(16, 32)):
            k.wait(scalar, "t", t1_halo + i + 1)
            hb = m - 16
            k.inc(scalar.copy(out=xcpre[:, hb * (TC + 3):hb * (TC + 3) + 3],
                              in_=gp_ps[0][:, 0:3]), "s", 1)

    xs_cons = [0, 0]
    st_tok = [0] * (NCH + 1)
    bank_tok = [k.tok("s"), k.tok("s")]
    conv_v = [0]
    conv_p = [0]
    prev_vn = [k.tok("vn")]
    prev_stat = [0]

    for c in range(NCH):
        @blk.sync
        def _(sync, c=c):
            if c >= 2:
                k.wait(sync, "vn", xs_cons[c % 2])
            src = io["xT"][:, c * TC:(c + 1) * TC].rearrange(
                "(pb p) t -> p pb t", p=128)
            k.inc(sync.dma_start(
                out=x_s[c % 2].ap().rearrange("p (pb t) -> p pb t", t=TC),
                in_=src), "q", 16)

        @blk.gpsimd
        def _(gpsimd, c=c):
            k.wait(gpsimd, "q", k.tok("q"))
            k.wait(gpsimd, "t", k.tok("t"))
            k.inc(gpsimd.tensor_copy(out=xb_s[:], in_=x_s[c % 2][:]), "p", 1)

        emit_norm(k, blk, W=TC, npb=8, onesb=onesb, onesrow=onesrow,
                  xb16=xb_s, sqscr=sq_s, rows=rows, bc_ps=bc_ps,
                  su_ps=su_ps, sq_ps=sq_ps, out=xn_s, xf32=x_s[c % 2],
                  scr=scr_v, eps=1e-5, with_mean=True,
                  pre=("p", k.tok("p")), tag="",
                  bc_war=prev_vn[0], stat_war=prev_stat[0],
                  out_war=([("qst", st_tok[c - 1])] if c >= 1 else []))
        xn_tok = k.tok("vn")
        prev_vn[0] = xn_tok
        prev_stat[0] = k.tok("v")
        xs_cons[c % 2] = xn_tok

        t1_b = k.tok("t")
        s1_b = k.tok("s")

        @blk.tensor
        def _(tensor, c=c, t1_b=t1_b, s1_b=s1_b):
            k.wait(tensor, "vn", xn_tok)
            for m in range(32):
                need = bank_tok[m % 2]
                if need:
                    k.wait(tensor, "s", need)
                bank_tok[m % 2] = s1_b + m + 1
                for kb in range(8):
                    ins = tensor.matmul(
                        out=gp_ps[m % 2][:],
                        lhsT=winT[kb][:, m * 128:(m + 1) * 128],
                        rhs=xn_s[:, kb * TC:(kb + 1) * TC],
                        start=(kb == 0), stop=(kb == 7))
                k.inc(ins, "t", 1)

        @blk.scalar
        def _(scalar, c=c, t1_b=t1_b):
            for m in range(32):
                k.wait(scalar, "t", t1_b + m + 1)
                if m == 0 and c >= 1:
                    k.wait(scalar, "qst", st_tok[c - 1])
                if m == 16 and c >= 1:
                    k.wait(scalar, "vc", conv_v[0])
                if m < 16:
                    ins2 = scalar.activation(out=gg_s[:, m * TC:(m + 1) * TC],
                                             in_=gp_ps[m % 2][:], func=AF.Gelu)
                else:
                    hb = m - 16
                    ins2 = scalar.copy(
                        out=xcpre[:, hb * (TC + 3) + 3:(hb + 1) * (TC + 3)],
                        in_=gp_ps[m % 2][:])
                k.inc(ins2, "s", 1)

        assert k.tok("s") == s1_b + 32

        def conv_body(eng, hbs, scrA, scrB, cn, c=c, s1_b=s1_b):
            for hb in hbs:
                k.wait(eng, "s", s1_b + 16 + hb + 1)
                if c >= 1:
                    k.wait(eng, "qst", st_tok[c - 1])
                base = hb * (TC + 3)
                w = lambda j: xcpre[:, base + j: base + j + TC]
                k.chain(eng, eng.tensor_scalar(out=scrA[:], in0=w(0),
                                  scalar1=cvw[:, hb * 4:hb * 4 + 1],
                                  scalar2=cvb[:, hb:hb + 1],
                                  op0=ALU.mult, op1=ALU.add), cn)
                k.chain(eng, eng.scalar_tensor_tensor(out=scrB[:], in0=w(1),
                                         scalar=cvw[:, hb * 4 + 1:hb * 4 + 2],
                                         in1=scrA[:], op0=ALU.mult, op1=ALU.add), cn)
                k.chain(eng, eng.scalar_tensor_tensor(out=scrA[:], in0=w(2),
                                         scalar=cvw[:, hb * 4 + 2:hb * 4 + 3],
                                         in1=scrB[:], op0=ALU.mult, op1=ALU.add), cn)
                ins = eng.scalar_tensor_tensor(out=xc_s[:, hb * TC:(hb + 1) * TC],
                                               in0=w(3),
                                               scalar=cvw[:, hb * 4 + 3:hb * 4 + 4],
                                               in1=scrA[:], op0=ALU.mult,
                                               op1=ALU.add)
                k.inc(ins, cn, 1)

        @blk.vector
        def _(vector, c=c):
            conv_body(vector, range(0, 16), cscrA, cscrB, "vc")

        conv_v[0] = k.tok("vc")
        conv_p[0] = 0

        if c < NCH - 1:
            @blk.scalar
            def _(scalar, c=c):
                k.wait(scalar, "vc", conv_v[0])
                for hb in range(16):
                    base = hb * (TC + 3)
                    ins = scalar.copy(out=xcpre[:, base:base + 3],
                                      in_=xcpre[:, base + TC:base + TC + 3])
                k.inc(ins, "s", 1)

        @blk.gpsimd
        def _(gpsimd, c=c, s1_b=s1_b):
            k.wait(gpsimd, "vn", xn_tok)
            k.inc(gpsimd.dma_start(
                out=io["XN"][:, c * TC:(c + 1) * TC].rearrange(
                    "(pb p) t -> p pb t", p=128),
                in_=xn_s.ap().rearrange("p (pb t) -> p pb t", t=TC)), "qst", 16)
            k.wait(gpsimd, "s", s1_b + 32)
            k.inc(gpsimd.dma_start(
                out=io["GG"][:, c * TC:(c + 1) * TC].rearrange(
                    "(pb p) t -> p pb t", p=128),
                in_=gg_s.ap().rearrange("p (pb t) -> p pb t", t=TC)), "qst", 16)
            k.wait(gpsimd, "vc", conv_v[0])
            k.inc(gpsimd.dma_start(
                out=io["XC"][:, c * TC:(c + 1) * TC].rearrange(
                    "(pb p) t -> p pb t", p=128),
                in_=xc_s.ap().rearrange("p (pb t) -> p pb t", t=TC)), "qst", 16)
        st_tok[c] = k.tok("qst")

    @blk.gpsimd
    def _(gpsimd):
        k.wait(gpsimd, "qst", k.tok("qst"))
        k.inc(gpsimd.tensor_copy(out=cscrC[:, 0:1], in_=cscrC[:, 0:1]), "p", 1)



def emit_ph2(k, blk, io):
    nc = k.nc
    TC = 512
    NCH = TCORE // TC
    es = k.es

    def sb(name, shape, dt):
        return es.enter_context(nc.sbuf_tensor(name, list(shape), dt))

    def ps(name, shape, dt=F32):
        return es.enter_context(nc.psum_tensor(name, list(shape), dt))

    wg = [sb(f"wg{i}", [128, 4096], BF16) for i in range(16)]
    stg = [sb(f"p2stg{i}", [128, 1024], F32) for i in range(2)]
    xc2 = sb("xc2", [128, 16 * TC], BF16)
    alpha = sb("alpha", [128, 8 * TC], F32)
    SCR1 = sb("p2scr1", [128, 2 * TC], F32)
    SCR2 = sb("p2scr2", [128, 2 * TC], F32)
    SCR3 = sb("p2scr3", [128, 2 * TC], F32)
    qb = sb("p2qb", [128, 8 * TC], BF16)
    Hb = sb("p2hb", [128, 8 * TC], BF16)
    Pb = sb("p2pb", [128, 8 * TC], BF16)
    hcar = sb("hcar", [128, 16], F32)
    pcar = sb("pcar", [128, 16], F32)
    zcol = sb("zcol", [128, 1], F32)
    bgc = sb("bgc", [128, 32], F32)
    hbg = sb("hbg", [128, 32], F32)
    fbc = sb("fbc", [128, 16], F32)
    sp8h = sb("sp8h", [128, 16], F32)
    t16a = sb("t16a", [128, 16], F32)
    t16b = sb("t16b", [128, 16], F32)

    gf_ps = [ps(f"gf_ps{i}", [128, TC]) for i in range(2)]
    gi_ps = [ps(f"gi_ps{i}", [128, TC]) for i in range(2)]

    @blk.sync
    def _(sync):
        k.inc(sync.dma_start(out=bgc[:], in_=io["bgcol"]), "q", 16)
        k.inc(sync.dma_start(out=fbc[:], in_=io["fbcol"]), "q", 16)
    c_tok = k.tok("q")

    wsems, wvals = emit_weight_load(k, blk, io["W_gT"], wg, stg,
                                    piece=1024, qname="q", cname="cw")

    @blk.vector
    def _(vector):
        k.wait(vector, "q", c_tok)
        vector.memset(hcar[:], 0.0)
        vector.memset(pcar[:], 1.0)
        vector.memset(zcol[:], 0.0)
        k.inc(vector.tensor_scalar_mul(out=hbg[:], in0=bgc[:], scalar1=0.5),
              "vi", 1)

    @blk.scalar
    def _(scalar):
        k.wait(scalar, "q", c_tok)
        k.chain(scalar, scalar.activation(out=t16a[:], in_=fbc[:], func=AF.Abs),
                "sch")
        k.inc(scalar.activation(out=t16a[:], in_=t16a[:], func=AF.Exp,
                                scale=-1.0), "si", 1)

    @blk.vector
    def _(vector):
        k.wait(vector, "si", k.tok("si"))
        k.inc(vector.tensor_scalar_add(out=t16b[:], in0=t16a[:], scalar1=1.0),
              "vi", 1)

    @blk.scalar
    def _(scalar):
        k.wait(scalar, "vi", k.tok("vi"))
        k.chain(scalar, scalar.activation(out=t16a[:], in_=t16b[:], func=AF.Ln),
                "sch")
        k.inc(scalar.activation(out=t16b[:], in_=fbc[:], func=AF.Relu), "si", 1)

    @blk.vector
    def _(vector):
        k.wait(vector, "si", k.tok("si"))
        k.chain(vector, vector.tensor_add(out=sp8h[:], in0=t16a[:], in1=t16b[:]),
                "vch")
        k.inc(vector.tensor_scalar_mul(out=sp8h[:], in0=sp8h[:], scalar1=-4.0),
              "vi", 1)
    init_v = k.tok("vi")

    st_tok = [0]
    fbank = [0, 0]
    ibank = [0, 0]
    xc_cons = [0]

    for c in range(NCH):
        @blk.sync
        def _(sync):
            if c >= 1:
                k.wait(sync, "p", xc_cons[0])
                k.wait(sync, "t", k.tok("t"))
            src = io["XC"][:, c * TC:(c + 1) * TC].rearrange(
                "(pb p) t -> p pb t", p=128)
            k.inc(sync.dma_start(
                out=xc2.ap().rearrange("p (pb t) -> p pb t", t=TC),
                in_=src), "q", 16)
        ld_tok = k.tok("q")

        for half in range(2):
            t2_b = k.tok("t")
            s2_b = k.tok("s")
            v2_b = k.tok("v")
            p2_b = k.tok("p")

            @blk.tensor
            def _(tensor):
                if c == 0 and half == 0:
                    k.wait(tensor, wsems[0], wvals[0])
                    k.wait(tensor, wsems[1], wvals[1])
                k.wait(tensor, "q", ld_tok)
                for pb in range(8):
                    pbg = half * 8 + pb
                    j = pb % 2
                    if fbank[j]:
                        k.wait(tensor, "s", fbank[j])
                    fbank[j] = s2_b + 3 * pb + 1
                    for kb in range(16):
                        ins = tensor.matmul(
                            out=gf_ps[j][:],
                            lhsT=wg[kb][:, pbg * 128:(pbg + 1) * 128],
                            rhs=xc2[:, kb * TC:(kb + 1) * TC],
                            start=(kb == 0), stop=(kb == 15))
                    k.inc(ins, "t", 1)
                    if ibank[j]:
                        k.wait(tensor, "s", ibank[j])
                    ibank[j] = s2_b + 3 * pb + 2
                    for kb in range(16):
                        ins = tensor.matmul(
                            out=gi_ps[j][:],
                            lhsT=wg[kb][:, 2048 + pbg * 128:2048 + (pbg + 1) * 128],
                            rhs=xc2[:, kb * TC:(kb + 1) * TC],
                            start=(kb == 0), stop=(kb == 15))
                    k.inc(ins, "t", 1)

            @blk.scalar
            def _(scalar):
                k.wait(scalar, "vi", init_v)
                k.wait(scalar, "v", v2_b)
                for pb in range(8):
                    pbg = half * 8 + pb
                    j = pb % 2
                    k.wait(scalar, "t", t2_b + 2 * pb + 1)
                    if pb >= 2:
                        k.wait(scalar, "p", p2_b + (pb - 2) + 1)
                    k.inc(scalar.activation(out=SCR1[:, j * TC:(j + 1) * TC],
                                            in_=gf_ps[j][:], func=AF.Tanh,
                                            scale=0.5,
                                            bias=hbg[:, pbg:pbg + 1]), "s", 1)
                    k.wait(scalar, "t", t2_b + 2 * pb + 2)
                    if pb >= 2:
                        k.wait(scalar, "v", v2_b + 2 * (pb - 2) + 2)
                    k.inc(scalar.activation(out=SCR3[:, j * TC:(j + 1) * TC],
                                            in_=gi_ps[j][:], func=AF.Tanh,
                                            scale=0.5,
                                            bias=hbg[:, 16 + pbg:17 + pbg]),
                          "s", 1)
                    k.wait(scalar, "v", v2_b + 2 * pb + 1)
                    k.inc(scalar.activation(out=alpha[:, pb * TC:(pb + 1) * TC],
                                            in_=SCR2[:, j * TC:(j + 1) * TC],
                                            func=AF.Exp), "s", 1)

            @blk.vector
            def _(vector):
                for pb in range(8):
                    pbg = half * 8 + pb
                    j = pb % 2
                    k.wait(vector, "s", s2_b + 3 * pb + 1)
                    k.inc(vector.tensor_scalar(
                        out=SCR2[:, j * TC:(j + 1) * TC],
                        in0=SCR1[:, j * TC:(j + 1) * TC],
                        scalar1=sp8h[:, pbg:pbg + 1],
                        scalar2=sp8h[:, pbg:pbg + 1],
                        op0=ALU.mult, op1=ALU.add), "v", 1)
                    k.wait(vector, "s", s2_b + 3 * pb + 2)
                    k.inc(vector.tensor_scalar(
                        out=SCR1[:, j * TC:(j + 1) * TC],
                        in0=SCR3[:, j * TC:(j + 1) * TC],
                        scalar1=0.5, scalar2=0.5,
                        op0=ALU.mult, op1=ALU.add), "v", 1)

            @blk.gpsimd
            def _(gpsimd):
                k.wait(gpsimd, "v", v2_b)
                for pb in range(8):
                    pbg = half * 8 + pb
                    j = pb % 2
                    k.wait(gpsimd, "v", v2_b + 2 * pb + 2)
                    k.inc(gpsimd.tensor_mul(
                        out=qb[:, pb * TC:(pb + 1) * TC],
                        in0=SCR1[:, j * TC:(j + 1) * TC],
                        in1=xc2[:, pbg * TC:(pbg + 1) * TC]), "p", 1)

            a_done_s = k.tok("s")
            a_done_v = k.tok("v")
            a_done_p = k.tok("p")
            sB_b = k.tok("s")
            vB_b = k.tok("v")

            @blk.scalar
            def _(scalar):
                k.wait(scalar, "v", a_done_v)
                k.wait(scalar, "p", a_done_p)
                if c >= 1 or half >= 1:
                    k.wait(scalar, "qst", st_tok[0])
                for pb in range(8):
                    j = pb % 2
                    if pb >= 2:
                        k.wait(scalar, "v", vB_b + 6 * (pb - 2) + 4)
                    k.inc(scalar.activation(out=SCR1[:, j * TC:(j + 1) * TC],
                                            in_=alpha[:, pb * TC:(pb + 1) * TC],
                                            func=AF.Square), "s", 1)
                    k.wait(scalar, "v", vB_b + 6 * pb + 1)
                    k.inc(scalar.activation(out=SCR3[:, j * TC:(j + 1) * TC],
                                            in_=SCR2[:, j * TC:(j + 1) * TC],
                                            func=AF.Sqrt), "s", 1)
                    k.wait(scalar, "v", vB_b + 6 * pb + 4)
                    k.inc(scalar.copy(out=Hb[:, pb * TC:(pb + 1) * TC],
                                      in_=SCR2[:, j * TC:(j + 1) * TC]), "s", 1)
                    k.wait(scalar, "v", vB_b + 6 * pb + 6)
                    k.inc(scalar.copy(out=Pb[:, pb * TC:(pb + 1) * TC],
                                      in_=SCR3[:, j * TC:(j + 1) * TC]), "s", 1)

            @blk.vector
            def _(vector):
                for pb in range(8):
                    pbg = half * 8 + pb
                    j = pb % 2
                    k.wait(vector, "s", sB_b + 4 * pb + 1)
                    k.inc(vector.tensor_scalar(
                        out=SCR2[:, j * TC:(j + 1) * TC],
                        in0=SCR1[:, j * TC:(j + 1) * TC],
                        scalar1=-1.0, scalar2=1.000001,
                        op0=ALU.mult, op1=ALU.add), "v", 1)
                    k.wait(vector, "s", sB_b + 4 * pb + 2)
                    k.inc(vector.tensor_mul(
                        out=SCR1[:, j * TC:(j + 1) * TC],
                        in0=SCR3[:, j * TC:(j + 1) * TC],
                        in1=qb[:, pb * TC:(pb + 1) * TC]), "v", 1)
                    k.wait(vector, "v")
                    k.chain(vector, vector.tensor_tensor_scan(
                        out=SCR2[:, j * TC:(j + 1) * TC],
                        data0=alpha[:, pb * TC:(pb + 1) * TC],
                        data1=SCR1[:, j * TC:(j + 1) * TC],
                        initial=hcar[:, pbg:pbg + 1],
                        op0=ALU.mult, op1=ALU.add), "vch")
                    k.inc(vector.tensor_copy(
                        out=hcar[:, pbg:pbg + 1],
                        in_=SCR2[:, (j + 1) * TC - 1:(j + 1) * TC]), "v", 2)
                    k.chain(vector, vector.tensor_tensor_scan(
                        out=SCR3[:, j * TC:(j + 1) * TC],
                        data0=alpha[:, pb * TC:(pb + 1) * TC],
                        data1=zcol[:].broadcast_to((128, TC)),
                        initial=pcar[:, pbg:pbg + 1],
                        op0=ALU.mult, op1=ALU.add), "vch")
                    k.inc(vector.tensor_copy(
                        out=pcar[:, pbg:pbg + 1],
                        in_=SCR3[:, (j + 1) * TC - 1:(j + 1) * TC]), "v", 2)

            assert k.tok("v") == vB_b + 48, (k.tok("v"), vB_b)
            assert k.tok("s") == sB_b + 32

            @blk.gpsimd
            def _(gpsimd):
                k.wait(gpsimd, "s", k.tok("s"))
                hoff = half * 8 * 128
                k.inc(gpsimd.dma_start(
                    out=io["H"][hoff:hoff + 1024, c * TC:(c + 1) * TC].rearrange(
                        "(pb p) t -> p pb t", p=128),
                    in_=Hb.ap().rearrange("p (pb t) -> p pb t", t=TC)), "qst", 16)
                k.inc(gpsimd.dma_start(
                    out=io["P"][hoff:hoff + 1024, c * TC:(c + 1) * TC].rearrange(
                        "(pb p) t -> p pb t", p=128),
                    in_=Pb.ap().rearrange("p (pb t) -> p pb t", t=TC)), "qst", 16)
            st_tok[0] = k.tok("qst")

        xc_cons[0] = k.tok("p")

    @blk.gpsimd
    def _(gpsimd):
        k.wait(gpsimd, "v", k.tok("v"))
        k.inc(gpsimd.dma_start(out=io["HEND"][:, 0:16], in_=hcar[:]), "hendq", 16)
        k.wait(gpsimd, "qst", k.tok("qst"))
        k.wait(gpsimd, "hendq", k.tok("hendq"))
        k.inc(gpsimd.tensor_copy(out=t16a[:, 0:1], in_=t16a[:, 0:1]), "p", 1)


def build_ph2_nc():
    nc = bass.Bass("TRN2", target_bir_lowering=False)
    io = {}
    for nm, shp, dt, kind in [
        ("XC", [2048, TCORE], BF16, "ExternalInput"),
        ("W_gT", [2048, 4096], F32, "ExternalInput"),
        ("bgcol", [128, 32], F32, "ExternalInput"),
        ("fbcol", [128, 16], F32, "ExternalInput"),
        ("H", [2048, TCORE], BF16, "ExternalOutput"),
        ("P", [2048, TCORE], BF16, "ExternalOutput"),
        ("HEND", [128, 128], F32, "ExternalOutput"),
    ]:
        io[nm] = nc.dram_tensor(nm, shp, dt, kind=kind).ap()
    with contextlib.ExitStack() as es:
        k = K(nc, es)
        blk = es.enter_context(nc.Block())
        emit_ph2(k, blk, io)
    return nc



def emit_ph3(k, blk, io):
    import os
    cut = os.environ.get("PH3_CUT", "full")
    nc = k.nc
    TC = 512
    NCH = TCORE // TC
    es = k.es

    def sb(name, shape, dt):
        return es.enter_context(nc.sbuf_tensor(name, list(shape), dt))

    def ps(name, shape, dt=F32):
        return es.enter_context(nc.psum_tensor(name, list(shape), dt))

    wo = [sb(f"wo{i}", [128, 1024], BF16) for i in range(16)]
    stg = [sb(f"p3stg{i}", [128, 1024], F32) for i in range(2)]
    ld = sb("ld3", [128, 8 * 128], BF16)
    ups = sb("ups3", [128, 1024], BF16)
    upsc = sb("upsc3", [128, 1024], BF16)
    rw = sb("rw3", [128, 64], BF16)
    ldf = sb("ldf3", [128, 1024], F32)
    upf = sb("upf3", [128, 1024], F32)
    rwf = sb("rwf3", [128, 64], F32)
    ind8 = sb("ind8_s", [8, 128], F32)
    id8 = sb("id8_s", [8, 8], F32)
    id1 = sb("id1_s", [1, 1], F32)
    onesb = sb("onesb3", [128, 1], BF16)
    onesrow = sb("onesrow3", [1, 128], F32)
    xt0 = sb("xt0", [128, 8], F32)
    xt0b = sb("xt0b", [128, 8], BF16)
    xt0q = sb("xt0q", [128, 8], BF16)
    xn0 = sb("xn0", [128, 8], BF16)
    grows = sb("grows", [1, 8 * 6], F32)
    g16s = sb("g16s", [128, 1], F32)
    nscr = sb("nscr3", [128, 1], F32)
    rows0 = sb("rows0", [1, 5], F32)
    hin = sb("hin3", [128, 16], F32)

    Hc = sb("Hc", [128, 16 * TC], BF16)
    Pc = sb("Pc", [128, 16 * TC], BF16)
    GGc = sb("GGc", [128, 16 * TC], BF16)
    XNc = sb("XNc", [128, 8 * TC], BF16)
    x3_s = sb("x3_s", [128, 8 * TC], F32)
    PR = sb("PR", [128, 16 * TC], BF16)
    hscr = sb("hscr3", [128, 2 * TC], F32)
    s_s = sb("s_s", [128, 8 * TC], BF16)
    dn_s = sb("dn_s", [128, TC], BF16)
    gscr = sb("gscr3", [128, 2 * TC], F32)
    g0b = sb("g0b3", [128, 8 * TC], BF16)
    x1o = sb("x1o", [128, 8 * TC], F32)

    s_ps = [ps(f"s_ps{i}", [128, TC]) for i in range(2)]
    dn_ps = ps("dn_ps", [128, TC])
    wei_ps = [ps(f"wei_ps{i}", [128, TC]) for i in range(2)]

    @blk.sync
    def _(sync):
        for nm, dst in [("ldT", ldf), ("upS", upf), ("rwT", rwf),
                        ("ind8", ind8), ("id8", id8), ("id1", id1),
                        ("onesb", onesb), ("onesrow", onesrow),
                        ("hin", hin)]:
            k.inc(sync.dma_start(out=dst[:], in_=io[nm]), "q", 16)
        k.inc(sync.dma_start(out=xt0[:], in_=io["xtok0"]), "q", 16)
    c_tok = k.tok("q")

    @blk.scalar
    def _(scalar):
        k.wait(scalar, "q", c_tok)
        scalar.copy(out=ld[:], in_=ldf[:])
        scalar.copy(out=ups[:], in_=upf[:])
        k.inc(scalar.copy(out=rw[:], in_=rwf[:]), "si", 1)

    wsems, wvals = emit_weight_load(k, blk, io["W_outT"], wo, stg,
                                    piece=1024, qname="q", cname="cw")

    if cut == "nopre":
        @blk.gpsimd
        def _(gpsimd):
            k.wait(gpsimd, "q", c_tok)
            k.inc(gpsimd.tensor_copy(out=xt0b[:], in_=xt0[:]), "p", 1)

        @blk.vector
        def _(vector):
            k.wait(vector, "si", k.tok("si"))
            k.inc(vector.tensor_copy(out=upsc[:], in_=ups[:]), "v", 1)
        pre_v = k.tok("v")
    else:
        @blk.gpsimd
        def _(gpsimd):
            k.wait(gpsimd, "q", c_tok)
            k.inc(gpsimd.tensor_copy(out=xt0b[:], in_=xt0[:]), "p", 1)

        emit_norm(k, blk, W=1, npb=8, onesb=onesb, onesrow=onesrow,
                  xb16=xt0b, sqscr=xt0q, rows=rows0,
                  bc_ps=wei_ps[1][:, 0:2], su_ps=dn_ps[0:1, 1:2],
                  sq_ps=dn_ps[0:1, 2:3], out=xn0, xf32=xt0,
                  scr=nscr, eps=1e-5, with_mean=True,
                  pre=("p", k.tok("p")), tag="")

        @blk.tensor
        def _(tensor):
            k.wait(tensor, "vn", k.tok("vn"))
            k.wait(tensor, "si", k.tok("si"))
            for kb in range(8):
                ins = tensor.matmul(out=dn_ps[0:8, 0:1],
                                    lhsT=rw[:, kb * 8:(kb + 1) * 8],
                                    rhs=xn0[:, kb:kb + 1],
                                    start=(kb == 0), stop=(kb == 7))
            k.inc(ins, "t", 1)

        @blk.scalar
        def _(scalar):
            k.wait(scalar, "t", k.tok("t"))
            k.inc(scalar.copy(out=nscr[0:8, 0:1], in_=dn_ps[0:8, 0:1]), "s", 1)

        @blk.tensor
        def _(tensor):
            k.wait(tensor, "s", k.tok("s"))
            k.inc(tensor.transpose(out=s_ps[0][0:1, 0:8], in_=nscr[0:8, 0:1],
                                   identity=id8[:]), "t", 1)

        @blk.vector
        def _(vector):
            k.wait(vector, "t", k.tok("t"))
            k.chain(vector, vector.tensor_copy(out=grows[:, 0:8],
                                               in_=s_ps[0][0:1, 0:8]), "vch")
            k.chain(vector, vector.reduce_max(out=grows[:, 40:41],
                                              in_=grows[:, 0:8],
                                              axis=mybir.AxisListType.X), "vch")
            k.chain(vector, vector.tensor_single_scalar(
                out=grows[:, 8:16], in_=grows[:, 0:8],
                scalar=grows[:, 40:41], op=ALU.is_equal), "vch")
            k.chain(vector, vector.scalar_tensor_tensor(
                out=grows[:, 8:16], in0=grows[:, 8:16], scalar=-1e30,
                in1=grows[:, 0:8], op0=ALU.mult, op1=ALU.add), "vch")
            k.chain(vector, vector.reduce_max(out=grows[:, 41:42],
                                              in_=grows[:, 8:16],
                                              axis=mybir.AxisListType.X), "vch")
            k.chain(vector, vector.tensor_single_scalar(
                out=grows[:, 16:24], in_=grows[:, 0:8],
                scalar=grows[:, 41:42], op=ALU.is_ge), "vch")
            k.inc(vector.tensor_scalar_mul(out=grows[:, 42:43],
                                           in0=grows[:, 40:41], scalar1=-1.0),
                  "v", 1)

        @blk.scalar
        def _(scalar):
            k.wait(scalar, "v", k.tok("v"))
            k.inc(scalar.activation(out=grows[:, 24:32], in_=grows[:, 0:8],
                                    func=AF.Exp, bias=grows[:, 42:43]), "s", 1)

        @blk.vector
        def _(vector):
            k.wait(vector, "s", k.tok("s"))
            k.chain(vector, vector.tensor_mul(out=grows[:, 24:32],
                                              in0=grows[:, 24:32],
                                              in1=grows[:, 16:24]), "vch")
            k.chain(vector, vector.reduce_sum(out=grows[:, 43:44],
                                              in_=grows[:, 24:32],
                                              axis=mybir.AxisListType.X), "vch")
            k.chain(vector, vector.reciprocal(out=grows[:, 44:45],
                                              in_=grows[:, 43:44]), "vch")
            k.inc(vector.tensor_single_scalar(out=grows[:, 32:40],
                                              in_=grows[:, 24:32],
                                              scalar=grows[:, 44:45],
                                              op=ALU.mult), "v", 1)

        @blk.tensor
        def _(tensor):
            k.wait(tensor, "v", k.tok("v"))
            k.inc(tensor.transpose(out=s_ps[1][0:8, 0:1], in_=grows[:, 32:40],
                                   identity=id1[:]), "t", 1)

        @blk.scalar
        def _(scalar):
            k.wait(scalar, "t", k.tok("t"))
            k.inc(scalar.copy(out=nscr[0:8, 0:1], in_=s_ps[1][0:8, 0:1]), "s", 1)

        @blk.tensor
        def _(tensor):
            k.wait(tensor, "s", k.tok("s"))
            k.inc(tensor.matmul(out=wei_ps[0][:, 0:1], lhsT=ind8[:],
                                rhs=nscr[0:8, 0:1], start=True, stop=True), "t", 1)

        @blk.scalar
        def _(scalar):
            k.wait(scalar, "t", k.tok("t"))
            k.inc(scalar.copy(out=g16s[:], in_=wei_ps[0][:, 0:1]), "s", 1)

        @blk.vector
        def _(vector):
            k.wait(vector, "s", k.tok("s"))
            k.inc(vector.tensor_scalar_mul(out=upsc[:], in0=ups[:],
                                           scalar1=g16s[:]), "v", 1)
        pre_v = k.tok("v")

    st_tok = [0]
    sbank = [0, 0]
    wbank = [0, 0]
    prcons = [0]

    for c in range(NCH):
        @blk.sync
        def _(sync):
            if c >= 1:
                k.wait(sync, "t", k.tok("t"))
                k.wait(sync, "p", k.tok("p"))
                k.wait(sync, "v", k.tok("v"))
                k.wait(sync, "vx", k.tok("vx"))
                k.wait(sync, "s", k.tok("s"))
            for nm, dst, npbx in [("H", Hc, 16), ("P", Pc, 16), ("GG", GGc, 16),
                                  ("XN", XNc, 8)]:
                k.inc(sync.dma_start(
                    out=dst.ap().rearrange("p (pb t) -> p pb t", t=TC),
                    in_=io[nm][:, c * TC:(c + 1) * TC].rearrange(
                        "(pb p) t -> p pb t", p=128)), "q", 16)
            k.inc(sync.dma_start(
                out=x3_s.ap().rearrange("p (pb t) -> p pb t", t=TC),
                in_=io["xT"][:, c * TC:(c + 1) * TC].rearrange(
                    "(pb p) t -> p pb t", p=128)), "q", 16)
        ld_tok = k.tok("q")

        v3_b = k.tok("v")
        p3_b = k.tok("p")

        @blk.vector
        def _(vector):
            k.wait(vector, "q", ld_tok)
            k.wait(vector, "v", pre_v)
            for pb in range(16):
                j = pb % 2
                if pb >= 2:
                    k.wait(vector, "p", p3_b + (pb - 2) + 1)
                k.inc(vector.scalar_tensor_tensor(
                    out=hscr[:, j * TC:(j + 1) * TC],
                    in0=Pc[:, pb * TC:(pb + 1) * TC],
                    scalar=hin[:, pb:pb + 1],
                    in1=Hc[:, pb * TC:(pb + 1) * TC],
                    op0=ALU.mult, op1=ALU.add), "v", 1)

        @blk.gpsimd
        def _(gpsimd):
            if c >= 1:
                k.wait(gpsimd, "t", prcons[0])
            for pb in range(16):
                j = pb % 2
                k.wait(gpsimd, "v", v3_b + pb + 1)
                k.inc(gpsimd.tensor_mul(
                    out=PR[:, pb * TC:(pb + 1) * TC],
                    in0=hscr[:, j * TC:(j + 1) * TC],
                    in1=GGc[:, pb * TC:(pb + 1) * TC]), "p", 1)

        pr_done = k.tok("p")
        t3_b = k.tok("t")
        s3_b = k.tok("s")

        @blk.tensor
        def _(tensor):
            if c == 0:
                k.wait(tensor, wsems[0], wvals[0])
                k.wait(tensor, wsems[1], wvals[1])
            k.wait(tensor, "p", pr_done)
            for m in range(8):
                j = m % 2
                if sbank[j]:
                    k.wait(tensor, "s", sbank[j])
                sbank[j] = s3_b + m + 1
                for kb in range(16):
                    ins = tensor.matmul(
                        out=s_ps[j][:],
                        lhsT=wo[kb][:, m * 128:(m + 1) * 128],
                        rhs=PR[:, kb * TC:(kb + 1) * TC],
                        start=(kb == 0), stop=(kb == 15))
                k.inc(ins, "t", 1)
            k.wait(tensor, "q", ld_tok)
            k.wait(tensor, "s", s3_b)
            for kb in range(8):
                ins = tensor.matmul(out=dn_ps[:],
                                    lhsT=ld[:, kb * 128:(kb + 1) * 128],
                                    rhs=XNc[:, kb * TC:(kb + 1) * TC],
                                    start=(kb == 0), stop=(kb == 7))
            k.inc(ins, "t", 1)

        @blk.scalar
        def _(scalar):
            if c >= 1:
                k.wait(scalar, "qst", st_tok[0])
                k.wait(scalar, "vx", k.tok("vx"))
            for m in range(8):
                j = m % 2
                k.wait(scalar, "t", t3_b + m + 1)
                k.inc(scalar.copy(out=s_s[:, m * TC:(m + 1) * TC],
                                  in_=s_ps[j][:]), "s", 1)
            k.wait(scalar, "t", t3_b + 9)
            k.inc(scalar.copy(out=dn_s[:], in_=dn_ps[:]), "s", 1)
        dn_tok = k.tok("s")

        if cut == "noffn":
            @blk.gpsimd
            def _(gpsimd):
                k.wait(gpsimd, "s", k.tok("s"))
                k.inc(gpsimd.dma_start(
                    out=io["X1"][:, c * TC:(c + 1) * TC].rearrange(
                        "(pb p) t -> p pb t", p=128),
                    in_=s_s.ap().rearrange("p (pb t) -> p pb t", t=TC)), "qst", 16)
            st_tok[0] = k.tok("qst")
            prcons[0] = k.tok("t")
            continue
        t3_c = k.tok("t")
        v3_c = k.tok("v")
        s3_c = k.tok("s")

        @blk.tensor
        def _(tensor):
            k.wait(tensor, "s", dn_tok)
            k.wait(tensor, "v", pre_v)
            for m in range(8):
                j = m % 2
                if wbank[j]:
                    k.wait(tensor, "v", wbank[j])
                wbank[j] = v3_c + m + 1
                ins = tensor.matmul(out=wei_ps[j][:],
                                    lhsT=upsc[:, m * 128:(m + 1) * 128],
                                    rhs=dn_s[:], start=True, stop=True)
                k.inc(ins, "t", 1)

        @blk.vector
        def _(vector):
            for m in range(8):
                j = m % 2
                k.wait(vector, "t", t3_c + m + 1)
                if m >= 2 and cut != "nog0":
                    k.wait(vector, "s", s3_c + (m - 2) + 1)
                k.inc(vector.tensor_add(out=gscr[:, j * TC:(j + 1) * TC],
                                        in0=XNc[:, m * TC:(m + 1) * TC],
                                        in1=wei_ps[j][:]), "v", 1)

        if cut == "nog0":
            @blk.gpsimd
            def _(gpsimd):
                k.wait(gpsimd, "v", k.tok("v"))
                k.inc(gpsimd.dma_start(
                    out=io["X1"][:, c * TC:(c + 1) * TC].rearrange(
                        "(pb p) t -> p pb t", p=128),
                    in_=s_s.ap().rearrange("p (pb t) -> p pb t", t=TC)), "qst", 16)
            st_tok[0] = k.tok("qst")
            prcons[0] = k.tok("t")
            continue

        v3x_c = k.tok("vx")

        @blk.scalar
        def _(scalar):
            for m in range(8):
                j = m % 2
                k.wait(scalar, "v", v3_c + m + 1)
                if c >= 1 and m == 0:
                    k.wait(scalar, "vx", v3x_c)
                k.inc(scalar.activation(out=g0b[:, m * TC:(m + 1) * TC],
                                        in_=gscr[:, j * TC:(j + 1) * TC],
                                        func=AF.Silu), "s", 1)

        if cut == "nocomb":
            @blk.gpsimd
            def _(gpsimd):
                k.wait(gpsimd, "s", k.tok("s"))
                k.inc(gpsimd.dma_start(
                    out=io["X1"][:, c * TC:(c + 1) * TC].rearrange(
                        "(pb p) t -> p pb t", p=128),
                    in_=s_s.ap().rearrange("p (pb t) -> p pb t", t=TC)), "qst", 16)
            st_tok[0] = k.tok("qst")
            prcons[0] = k.tok("t")
            continue

        @blk.vector
        def _(vector):
            k.wait(vector, "p", pr_done)
            if c >= 1:
                k.wait(vector, "qst", st_tok[0])
            for m in range(8):
                j = m % 2
                k.wait(vector, "s", s3_c + m + 1)
                k.chain(vector, vector.tensor_mul(
                    out=hscr[:, 0:TC],
                    in0=s_s[:, m * TC:(m + 1) * TC],
                    in1=g0b[:, m * TC:(m + 1) * TC]), "vch")
                k.inc(vector.tensor_add(out=x1o[:, m * TC:(m + 1) * TC],
                                        in0=x3_s[:, m * TC:(m + 1) * TC],
                                        in1=hscr[:, 0:TC]), "vx", 1)

        prcons[0] = k.tok("t")

        @blk.gpsimd
        def _(gpsimd):
            k.wait(gpsimd, "vx", k.tok("vx"))
            k.inc(gpsimd.dma_start(
                out=io["X1"][:, c * TC:(c + 1) * TC].rearrange(
                    "(pb p) t -> p pb t", p=128),
                in_=x1o.ap().rearrange("p (pb t) -> p pb t", t=TC)), "qst", 16)
        st_tok[0] = k.tok("qst")

    @blk.gpsimd
    def _(gpsimd):
        k.wait(gpsimd, "qst", k.tok("qst"))
        k.inc(gpsimd.tensor_copy(out=g16s[:, 0:1], in_=g16s[:, 0:1]), "p", 1)


def build_ph3_nc():
    nc = bass.Bass("TRN2", target_bir_lowering=False)
    io = {}
    for nm, shp, dt, kind in [
        ("H", [2048, TCORE], BF16, "ExternalInput"),
        ("P", [2048, TCORE], BF16, "ExternalInput"),
        ("GG", [2048, TCORE], BF16, "ExternalInput"),
        ("XN", [1024, TCORE], BF16, "ExternalInput"),
        ("xT", [1024, TCORE], F32, "ExternalInput"),
        ("hin", [128, 16], F32, "ExternalInput"),
        ("W_outT", [2048, 1024], F32, "ExternalInput"),
        ("ldT", [128, 1024], F32, "ExternalInput"),
        ("upS", [128, 1024], F32, "ExternalInput"),
        ("rwT", [128, 64], F32, "ExternalInput"),
        ("ind8", [8, 128], F32, "ExternalInput"),
        ("id8", [8, 8], F32, "ExternalInput"),
        ("id1", [1, 1], F32, "ExternalInput"),
        ("onesb", [128, 1], BF16, "ExternalInput"),
        ("onesrow", [1, 128], F32, "ExternalInput"),
        ("xtok0", [128, 8], F32, "ExternalInput"),
        ("X1", [1024, TCORE], F32, "ExternalOutput"),
    ]:
        io[nm] = nc.dram_tensor(nm, shp, dt, kind=kind).ap()
    with contextlib.ExitStack() as es:
        k = K(nc, es)
        blk = es.enter_context(nc.Block())
        emit_ph3(k, blk, io)
    return nc



def emit_ph4(k, blk, io):
    import os
    cut = os.environ.get("PH4_CUT", "full")
    nc = k.nc
    TC = 512
    NCH = TCORE // TC
    es = k.es

    def sb(name, shape, dt):
        return es.enter_context(nc.sbuf_tensor(name, list(shape), dt))

    def ps(name, shape, dt=F32):
        return es.enter_context(nc.psum_tensor(name, list(shape), dt))

    wk = [sb(f"wk4{i}", [128, 1024], BF16) for i in range(8)]
    wv = [sb(f"wv4{i}", [128, 1024], BF16) for i in range(8)]
    wr = [sb(f"wr4{i}", [128, 1024], BF16) for i in range(8)]
    stg = [sb(f"p4stg{i}", [128, 1024], F32) for i in range(2)]
    x4_s = sb("x4_s", [128, 8 * TC], F32)
    x4b = sb("x4b", [128, 8 * TC], BF16)
    sq4 = sb("sq4", [128, 8 * TC], BF16)
    xn1_s = sb("xn1_s", [128, 8 * TC], BF16)
    nm_s = sb("nm_s", [128, 8 * TC], BF16)
    kv_s = sb("kv_s", [128, 8 * TC], F32)
    S_s = sb("S_s", [128, 8 * (TC + 1)], F32)
    wkvb = sb("wkvb", [128, 8 * TC], BF16)
    rb4 = sb("rb4", [128, 8 * TC], BF16)
    kcp = sb("kcp", [128, 2 * TC], F32)
    vcp = sb("vcp", [128, 2 * TC], F32)
    thr = sb("thr", [128, 2 * TC], F32)
    rows4 = sb("p4rows", [1, 5 * TC], F32)
    rows4b = sb("p4rowsb", [1, 5 * TC], F32)
    scr4 = sb("p4scr", [128, TC], F32)
    onesb = sb("onesb4", [128, 1], BF16)
    onesrow = sb("onesrow4", [1, 128], F32)
    tdc = sb("tdc", [128, 8], F32)
    tfc = sb("tfc", [128, 8], F32)
    ewc = sb("ewc", [128, 8], F32)
    euc = sb("euc", [128, 8], F32)
    send = sb("send4", [128, 8], F32)

    kp_ps = ps("kp_ps", [128, TC])
    vp_ps = ps("vp_ps", [128, TC])
    rp_ps = ps("rp_ps", [128, TC])
    su_ps = ps("su4_ps", [1, TC])
    sq_ps = ps("sq4_ps", [1, TC])
    bc_ps = ps("bc4_ps", [128, 2 * TC])

    @blk.sync
    def _(sync):
        k.inc(sync.dma_start(out=onesb[:], in_=io["onesb"]), "q", 16)
        k.inc(sync.dma_start(out=onesrow[:], in_=io["onesrow"]), "q", 16)
        k.inc(sync.dma_start(out=tdc[:], in_=io["tdcol"]), "q", 16)
        k.inc(sync.dma_start(out=tfc[:], in_=io["tfcol"]), "q", 16)
    c_tok = k.tok("q")

    ws1, wv1 = emit_weight_load(k, blk, io["WkT"], wk, stg, piece=1024,
                                qname="q", cname="cw")
    ws2, wv2 = emit_weight_load(k, blk, io["WvT"], wv, stg, piece=1024,
                                qname="q", cname="cw", prev=(ws1, wv1))
    ws3, wv3 = emit_weight_load(k, blk, io["WrT"], wr, stg, piece=1024,
                                qname="q", cname="cw", prev=(ws2, wv2))

    @blk.scalar
    def _(scalar):
        k.wait(scalar, "q", c_tok)
        k.chain(scalar, scalar.activation(out=ewc[:], in_=tdc[:], func=AF.Exp),
                "sch")
        k.chain(scalar, scalar.activation(out=ewc[:], in_=ewc[:], func=AF.Exp,
                                          scale=-1.0), "sch")
        k.inc(scalar.activation(out=euc[:], in_=tfc[:], func=AF.Exp), "si", 1)

    @blk.vector
    def _(vector):
        k.inc(vector.memset(S_s[:], 0.0), "vi", 1)
    init_v = k.tok("vi")
    init_s = k.tok("si")

    st_tok = [0]
    prev_norm_vn = [0]
    prev_norm_v = [0]

    for c in range(NCH):
        @blk.sync
        def _(sync):
            if c >= 1:
                k.wait(sync, "t", k.tok("t"))
                k.wait(sync, "v", k.tok("v"))
                k.wait(sync, "vn", k.tok("vn"))
                k.wait(sync, "p", k.tok("p"))
                k.wait(sync, "s", k.tok("s"))
            k.inc(sync.dma_start(
                out=x4_s.ap().rearrange("p (pb t) -> p pb t", t=TC),
                in_=io["X1"][:, c * TC:(c + 1) * TC].rearrange(
                    "(pb p) t -> p pb t", p=128)), "q", 16)
        ld_tok = k.tok("q")

        @blk.gpsimd
        def _(gpsimd):
            k.wait(gpsimd, "q", ld_tok)
            k.wait(gpsimd, "t", k.tok("t"))
            k.inc(gpsimd.tensor_copy(out=x4b[:], in_=x4_s[:]), "p", 1)

        emit_norm(k, blk, W=TC, npb=8, onesb=onesb, onesrow=onesrow,
                  xb16=x4b, sqscr=sq4, rows=rows4, bc_ps=bc_ps,
                  su_ps=su_ps, sq_ps=sq_ps, out=xn1_s, xf32=x4_s,
                  scr=scr4, eps=1e-5, with_mean=True,
                  pre=("p", k.tok("p")), tag="",
                  bc_war=prev_norm_vn[0], stat_war=prev_norm_v[0],
                  out_war=([("qst", st_tok[0])] if c >= 1 else []))
        xn1_tok = k.tok("vn")
        post_ln_v = k.tok("v")

        emit_norm(k, blk, W=TC, npb=8, onesb=onesb, onesrow=onesrow,
                  xb16=xn1_s, sqscr=sq4, rows=rows4b, bc_ps=bc_ps,
                  su_ps=su_ps, sq_ps=sq_ps, out=nm_s, xf32=None,
                  scr=scr4, eps=1e-6, with_mean=False,
                  pre=("vn", xn1_tok), tag="",
                  bc_war=xn1_tok, stat_war=post_ln_v)
        nm_tok = k.tok("vn")
        prev_norm_vn[0] = nm_tok
        prev_norm_v[0] = k.tok("v")

        @blk.gpsimd
        def _(gpsimd):
            k.wait(gpsimd, "vn", xn1_tok)
            k.inc(gpsimd.dma_start(
                out=io["XN1"][:, c * TC:(c + 1) * TC].rearrange(
                    "(pb p) t -> p pb t", p=128),
                in_=xn1_s.ap().rearrange("p (pb t) -> p pb t", t=TC)), "qst", 16)

        if cut == "norms":
            @blk.gpsimd
            def _(gpsimd):
                k.wait(gpsimd, "vn", nm_tok)
                k.inc(gpsimd.dma_start(
                    out=io["WKV"][:, c * TC:(c + 1) * TC].rearrange(
                        "(pb p) t -> p pb t", p=128),
                    in_=nm_s.ap().rearrange("p (pb t) -> p pb t", t=TC)),
                    "qst", 16)
                k.inc(gpsimd.dma_start(
                    out=io["R"][:, c * TC:(c + 1) * TC].rearrange(
                        "(pb p) t -> p pb t", p=128),
                    in_=nm_s.ap().rearrange("p (pb t) -> p pb t", t=TC)),
                    "qst", 16)
            st_tok[0] = k.tok("qst")
            continue
        t4_b = k.tok("t")
        s4_b = k.tok("s")
        v4_b = k.tok("v")

        @blk.tensor
        def _(tensor):
            if c == 0:
                for wsx, wvx in [(ws1, wv1), (ws2, wv2), (ws3, wv3)]:
                    k.wait(tensor, wsx[0], wvx[0])
                    k.wait(tensor, wsx[1], wvx[1])
            k.wait(tensor, "vn", nm_tok)
            for m in range(8):
                if m >= 1:
                    k.wait(tensor, "s", s4_b + 3 * (m - 1) + 1)
                for kb in range(8):
                    ins = tensor.matmul(out=kp_ps[:],
                                        lhsT=wk[kb][:, m * 128:(m + 1) * 128],
                                        rhs=nm_s[:, kb * TC:(kb + 1) * TC],
                                        start=(kb == 0), stop=(kb == 7))
                k.inc(ins, "t", 1)
                if m >= 1:
                    k.wait(tensor, "s", s4_b + 3 * (m - 1) + 2)
                for kb in range(8):
                    ins = tensor.matmul(out=vp_ps[:],
                                        lhsT=wv[kb][:, m * 128:(m + 1) * 128],
                                        rhs=nm_s[:, kb * TC:(kb + 1) * TC],
                                        start=(kb == 0), stop=(kb == 7))
                k.inc(ins, "t", 1)
                if m >= 1:
                    k.wait(tensor, "s", s4_b + 3 * (m - 1) + 3)
                for kb in range(8):
                    ins = tensor.matmul(out=rp_ps[:],
                                        lhsT=wr[kb][:, m * 128:(m + 1) * 128],
                                        rhs=nm_s[:, kb * TC:(kb + 1) * TC],
                                        start=(kb == 0), stop=(kb == 7))
                k.inc(ins, "t", 1)

        @blk.scalar
        def _(scalar):
            k.wait(scalar, "si", init_s)
            for m in range(8):
                j = m % 2
                k.wait(scalar, "t", t4_b + 3 * m + 1)
                if m >= 2:
                    k.wait(scalar, "v", v4_b + 4 * (m - 2) + 1)
                k.inc(scalar.copy(out=kcp[:, j * TC:(j + 1) * TC],
                                  in_=kp_ps[:]), "s", 1)
                k.wait(scalar, "t", t4_b + 3 * m + 2)
                k.inc(scalar.copy(out=vcp[:, j * TC:(j + 1) * TC],
                                  in_=vp_ps[:]), "s", 1)
                k.wait(scalar, "t", t4_b + 3 * m + 3)
                if m >= 2:
                    k.wait(scalar, "v", v4_b + 4 * (m - 2) + 4)
                k.inc(scalar.activation(out=thr[:, j * TC:(j + 1) * TC],
                                        in_=rp_ps[:], func=AF.Tanh, scale=0.5),
                      "s", 1)

        @blk.vector
        def _(vector):
            k.wait(vector, "vi", init_v)
            k.wait(vector, "si", init_s)
            if c >= 1:
                k.wait(vector, "qst", st_tok[0])
            for m in range(8):
                j = m % 2
                k.wait(vector, "s", s4_b + 3 * m + 2)
                k.inc(vector.tensor_mul(out=kv_s[:, m * TC:(m + 1) * TC],
                                        in0=kcp[:, j * TC:(j + 1) * TC],
                                        in1=vcp[:, j * TC:(j + 1) * TC]), "v", 1)
                k.wait(vector, "v")
                base = m * (TC + 1)
                if cut != "noscan":
                    k.chain(vector, vector.tensor_tensor_scan(
                        out=S_s[:, base + 1:base + 1 + TC],
                        data0=ewc[:, m:m + 1].broadcast_to((128, TC)),
                        data1=kv_s[:, m * TC:(m + 1) * TC],
                        initial=S_s[:, base:base + 1],
                        op0=ALU.mult, op1=ALU.add), "vch")
                    k.inc(vector.scalar_tensor_tensor(
                        out=wkvb[:, m * TC:(m + 1) * TC],
                        in0=kv_s[:, m * TC:(m + 1) * TC],
                        scalar=euc[:, m:m + 1],
                        in1=S_s[:, base:base + TC],
                        op0=ALU.mult, op1=ALU.add), "v", 1)
                    k.wait(vector, "v")
                    k.inc(vector.tensor_copy(
                        out=S_s[:, base:base + 1],
                        in_=S_s[:, base + TC:base + TC + 1]), "v", 1)
                elif cut == "kdump":
                    k.inc(vector.tensor_copy(
                        out=wkvb[:, m * TC:(m + 1) * TC],
                        in_=kcp[:, j * TC:(j + 1) * TC]), "v", 1)
                    k.inc(vector.tensor_copy(out=S_s[:, base:base + 1],
                                             in_=S_s[:, base:base + 1]), "v", 1)
                else:
                    k.inc(vector.tensor_copy(
                        out=wkvb[:, m * TC:(m + 1) * TC],
                        in_=kv_s[:, m * TC:(m + 1) * TC]), "v", 1)
                    k.inc(vector.tensor_copy(out=S_s[:, base:base + 1],
                                             in_=S_s[:, base:base + 1]), "v", 1)
                k.wait(vector, "s", s4_b + 3 * m + 3)
                if cut == "kdump":
                    k.inc(vector.tensor_copy(out=rb4[:, m * TC:(m + 1) * TC],
                                             in_=vcp[:, j * TC:(j + 1) * TC]),
                          "v", 1)
                else:
                    k.inc(vector.tensor_scalar(out=rb4[:, m * TC:(m + 1) * TC],
                                               in0=thr[:, j * TC:(j + 1) * TC],
                                               scalar1=0.5, scalar2=0.5,
                                               op0=ALU.mult, op1=ALU.add), "v", 1)

        @blk.gpsimd
        def _(gpsimd):
            k.wait(gpsimd, "v", k.tok("v"))
            k.inc(gpsimd.dma_start(
                out=io["WKV"][:, c * TC:(c + 1) * TC].rearrange(
                    "(pb p) t -> p pb t", p=128),
                in_=wkvb.ap().rearrange("p (pb t) -> p pb t", t=TC)), "qst", 16)
            k.inc(gpsimd.dma_start(
                out=io["R"][:, c * TC:(c + 1) * TC].rearrange(
                    "(pb p) t -> p pb t", p=128),
                in_=rb4.ap().rearrange("p (pb t) -> p pb t", t=TC)), "qst", 16)
        st_tok[0] = k.tok("qst")

    @blk.vector
    def _(vector):
        for m in range(8):
            k.inc(vector.tensor_copy(out=send[:, m:m + 1],
                                     in_=S_s[:, m * (TC + 1):m * (TC + 1) + 1]),
                  "v", 1)

    @blk.gpsimd
    def _(gpsimd):
        k.wait(gpsimd, "v", k.tok("v"))
        k.inc(gpsimd.dma_start(out=io["SEND"][:, 0:8], in_=send[:]), "sendq", 16)
        k.wait(gpsimd, "qst", k.tok("qst"))
        k.wait(gpsimd, "sendq", k.tok("sendq"))
        k.inc(gpsimd.tensor_copy(out=send[:, 0:1], in_=send[:, 0:1]), "p", 1)


def build_ph4_nc():
    nc = bass.Bass("TRN2", target_bir_lowering=False)
    io = {}
    for nm, shp, dt, kind in [
        ("X1", [1024, TCORE], F32, "ExternalInput"),
        ("WkT", [1024, 1024], F32, "ExternalInput"),
        ("WvT", [1024, 1024], F32, "ExternalInput"),
        ("WrT", [1024, 1024], F32, "ExternalInput"),
        ("tdcol", [128, 8], F32, "ExternalInput"),
        ("tfcol", [128, 8], F32, "ExternalInput"),
        ("onesb", [128, 1], BF16, "ExternalInput"),
        ("onesrow", [1, 128], F32, "ExternalInput"),
        ("XN1", [1024, TCORE], BF16, "ExternalOutput"),
        ("WKV", [1024, TCORE], BF16, "ExternalOutput"),
        ("R", [1024, TCORE], BF16, "ExternalOutput"),
        ("SEND", [128, 128], F32, "ExternalOutput"),
    ]:
        io[nm] = nc.dram_tensor(nm, shp, dt, kind=kind).ap()
    with contextlib.ExitStack() as es:
        k = K(nc, es)
        blk = es.enter_context(nc.Block())
        emit_ph4(k, blk, io)
    return nc



def emit_ph5a(k, blk, io):
    nc = k.nc
    TC = 512
    NCH = TCORE // TC
    es = k.es

    def sb(name, shape, dt):
        return es.enter_context(nc.sbuf_tensor(name, list(shape), dt))

    def ps(name, shape, dt=F32):
        return es.enter_context(nc.psum_tensor(name, list(shape), dt))

    wo1 = [sb(f"wo1_{i}", [128, 1024], BF16) for i in range(8)]
    stg = [sb(f"p5astg{i}", [128, 1024], F32) for i in range(2)]
    rwrf = sb("rwrf", [128, 128], F32)
    rwr = sb("rwr", [128, 128], BF16)
    id128b = sb("id128b_s", [128, 128], BF16)
    xn5 = sb("xn5", [128, 8 * TC], BF16)
    wkv5 = sb("wkv5", [128, 8 * TC], BF16)
    r5 = sb("r5", [128, 8 * TC], BF16)
    mxb = sb("mxb", [128, 8 * TC], BF16)
    ewts = sb("ewts", [128, 2 * TC], F32)
    wkt = sb("wkt", [128, 2 * TC], F32)
    xmo = sb("xmo", [128, 8 * TC], F32)
    tdrow = sb("tdrow_s", [1, 1024], F32)
    wrow = sb("wrow", [1, 1024], F32)
    iota = sb("iota5", [1, 2048], F32)
    sin5 = sb("sin5", [128, 8], F32)
    ms5 = sb("ms5", [128, 8], F32)
    crow = sb("crow", [128, 8], F32)
    keepb = sb("keepb", [128, 16], F32)
    exb = sb("exb", [128, 16], F32)
    l2b = sb("l2b", [128, 16], F32)
    cwb = sb("cwb", [128, 16], BF16)
    cts = sb("cts", [16, 128], BF16)

    ewt_ps = ps("ewt_ps", [128, TC])
    mo_ps = [ps(f"mo_ps{i}", [128, TC]) for i in range(2)]
    lg_ps = ps("lg_ps", [128, 16])
    ct_ps = ps("ct_ps", [16, 128], BF16)

    @blk.sync
    def _(sync):
        for nm, dst in [("tdrow", tdrow), ("iota", iota), ("sin", sin5),
                        ("mscol", ms5), ("rwrT", rwrf), ("id128b", id128b)]:
            k.inc(sync.dma_start(out=dst[:], in_=io[nm]), "q", 16)
    c_tok = k.tok("q")

    wsems, wvals = emit_weight_load(k, blk, io["WoT"], wo1, stg,
                                    piece=1024, qname="q", cname="cw")

    @blk.scalar
    def _(scalar):
        k.wait(scalar, "q", c_tok)
        k.inc(scalar.copy(out=rwr[:], in_=rwrf[:]), "si", 1)
        k.inc(scalar.activation(out=wrow[:], in_=tdrow[:], func=AF.Exp),
              "si", 1)

    @blk.vector
    def _(vector):
        k.wait(vector, "si", k.tok("si"))
        k.inc(vector.tensor_scalar_mul(out=wrow[:], in0=wrow[:], scalar1=-1.0),
              "vi", 1)
    init_v = k.tok("vi")

    st_tok = [0]
    ctst = [0]
    mobank = [0, 0]

    for c in range(NCH):
        @blk.sync
        def _(sync):
            if c >= 1:
                k.wait(sync, "t", k.tok("t"))
                k.wait(sync, "v", k.tok("v"))
                k.wait(sync, "p", k.tok("p"))
            for nm, dst in [("XN1", xn5), ("WKV", wkv5), ("R", r5)]:
                k.inc(sync.dma_start(
                    out=dst.ap().rearrange("p (pb t) -> p pb t", t=TC),
                    in_=io[nm][:, c * TC:(c + 1) * TC].rearrange(
                        "(pb p) t -> p pb t", p=128)), "q", 16)
        ld_tok = k.tok("q")

        t5_b = k.tok("t")
        s5_b = k.tok("s")
        v5_b = k.tok("v")
        p5_b = k.tok("p")

        @blk.tensor
        def _(tensor):
            k.wait(tensor, "vi", init_v)
            k.wait(tensor, "q", ld_tok)
            for pb in range(8):
                if c > 0 or pb >= 1:
                    k.wait(tensor, "s", s5_b + pb if pb >= 1 else k.tok("s"))
                ins = tensor.matmul(out=ewt_ps[:],
                                    lhsT=wrow[:, pb * 128:(pb + 1) * 128],
                                    rhs=iota[:, c * TC:(c + 1) * TC],
                                    start=True, stop=True)
                k.inc(ins, "t", 1)

        @blk.scalar
        def _(scalar):
            for pb in range(8):
                j = pb % 2
                k.wait(scalar, "t", t5_b + pb + 1)
                if pb >= 2:
                    k.wait(scalar, "v", v5_b + (pb - 2) + 1)
                k.inc(scalar.activation(out=ewts[:, j * TC:(j + 1) * TC],
                                        in_=ewt_ps[:], func=AF.Exp), "s", 1)

        @blk.vector
        def _(vector):
            for pb in range(8):
                j = pb % 2
                k.wait(vector, "s", s5_b + pb + 1)
                if pb >= 2:
                    k.wait(vector, "p", p5_b + (pb - 2) + 1)
                k.inc(vector.scalar_tensor_tensor(
                    out=wkt[:, j * TC:(j + 1) * TC],
                    in0=ewts[:, j * TC:(j + 1) * TC],
                    scalar=sin5[:, pb:pb + 1],
                    in1=wkv5[:, pb * TC:(pb + 1) * TC],
                    op0=ALU.mult, op1=ALU.add), "v", 1)

        @blk.gpsimd
        def _(gpsimd):
            if c >= 1:
                k.wait(gpsimd, "t", t5_b)
            for pb in range(8):
                j = pb % 2
                k.wait(gpsimd, "v", v5_b + pb + 1)
                k.inc(gpsimd.tensor_mul(
                    out=mxb[:, pb * TC:(pb + 1) * TC],
                    in0=wkt[:, j * TC:(j + 1) * TC],
                    in1=r5[:, pb * TC:(pb + 1) * TC]), "p", 1)
        mx_done = k.tok("p")

        t5_c = k.tok("t")
        v5_c = k.tok("v")

        @blk.tensor
        def _(tensor):
            if c == 0:
                k.wait(tensor, wsems[0], wvals[0])
                k.wait(tensor, wsems[1], wvals[1])
            k.wait(tensor, "p", mx_done)
            for m in range(8):
                j = m % 2
                if mobank[j]:
                    k.wait(tensor, "v", mobank[j])
                mobank[j] = v5_c + m + 1
                for kb in range(8):
                    ins = tensor.matmul(out=mo_ps[j][:],
                                        lhsT=wo1[kb][:, m * 128:(m + 1) * 128],
                                        rhs=mxb[:, kb * TC:(kb + 1) * TC],
                                        start=(kb == 0), stop=(kb == 7))
                k.inc(ins, "t", 1)

        @blk.vector
        def _(vector):
            if c >= 1:
                k.wait(vector, "qst", st_tok[0])
            for m in range(8):
                j = m % 2
                k.wait(vector, "t", t5_c + m + 1)
                k.inc(vector.scalar_tensor_tensor(
                    out=xmo[:, m * TC:(m + 1) * TC],
                    in0=mo_ps[j][:], scalar=ms5[:, m:m + 1],
                    in1=xn5[:, m * TC:(m + 1) * TC],
                    op0=ALU.mult, op1=ALU.add), "v", 1)

        @blk.gpsimd
        def _(gpsimd):
            k.wait(gpsimd, "v", k.tok("v"))
            k.inc(gpsimd.dma_start(
                out=io["XMID"][:, c * TC:(c + 1) * TC].rearrange(
                    "(pb p) t -> p pb t", p=128),
                in_=xmo.ap().rearrange("p (pb t) -> p pb t", t=TC)), "qst", 16)
        st_tok[0] = k.tok("qst")

        for sc in range(4):
            t5_r = k.tok("t")

            @blk.tensor
            def _(tensor, sc=sc):
                if c > 0 or sc > 0:
                    k.wait(tensor, "s", k.tok("s"))
                    k.wait(tensor, "v", k.tok("v"))
                for kb in range(8):
                    ins = tensor.matmul(
                        out=lg_ps[:],
                        lhsT=xn5[:, kb * TC + sc * 128:kb * TC + sc * 128 + 128],
                        rhs=rwr[:, kb * 16:(kb + 1) * 16],
                        start=(kb == 0), stop=(kb == 7))
                k.inc(ins, "t", 1)

            @blk.vector
            def _(vector, sc=sc):
                k.wait(vector, "t", k.tok("t"))
                k.chain(vector, vector.reduce_max(
                    out=crow[:, 0:1], in_=lg_ps[:],
                    axis=mybir.AxisListType.X), "vch")
                k.chain(vector, vector.tensor_single_scalar(
                    out=keepb[:], in_=lg_ps[:], scalar=crow[:, 0:1],
                    op=ALU.is_equal), "vch")
                k.chain(vector, vector.scalar_tensor_tensor(
                    out=l2b[:], in0=keepb[:], scalar=-1e30, in1=lg_ps[:],
                    op0=ALU.mult, op1=ALU.add), "vch")
                k.chain(vector, vector.reduce_max(
                    out=crow[:, 1:2], in_=l2b[:],
                    axis=mybir.AxisListType.X), "vch")
                k.chain(vector, vector.tensor_single_scalar(
                    out=keepb[:], in_=lg_ps[:], scalar=crow[:, 1:2],
                    op=ALU.is_ge), "vch")
                k.inc(vector.tensor_scalar_mul(
                    out=crow[:, 2:3], in0=crow[:, 0:1], scalar1=-1.0), "v", 1)

            @blk.scalar
            def _(scalar, sc=sc):
                k.wait(scalar, "v", k.tok("v"))
                k.inc(scalar.activation(out=exb[:], in_=lg_ps[:], func=AF.Exp,
                                        bias=crow[:, 2:3]), "s", 1)

            @blk.vector
            def _(vector, sc=sc):
                k.wait(vector, "s", k.tok("s"))
                k.chain(vector, vector.reduce_sum(
                    out=crow[:, 3:4], in_=exb[:],
                    axis=mybir.AxisListType.X), "vch")
                k.chain(vector, vector.reciprocal(
                    out=crow[:, 4:5], in_=crow[:, 3:4]), "vch")
                k.chain(vector, vector.scalar_tensor_tensor(
                    out=cwb[:], in0=exb[:], scalar=crow[:, 4:5], in1=keepb[:],
                    op0=ALU.mult, op1=ALU.mult), "vch")
                k.inc(vector.tensor_copy(out=cwb[:, 0:1], in_=cwb[:, 0:1]),
                      "v", 1)

            @blk.tensor
            def _(tensor, sc=sc):
                k.wait(tensor, "v", k.tok("v"))
                if c > 0 or sc > 0:
                    k.wait(tensor, "s", k.tok("s"))
                k.inc(tensor.transpose(out=ct_ps[:], in_=cwb[:],
                                       identity=id128b[:]), "t", 1)

            @blk.scalar
            def _(scalar, sc=sc):
                k.wait(scalar, "t", k.tok("t"))
                k.wait(scalar, "qst", ctst[0])
                k.inc(scalar.copy(out=cts[:], in_=ct_ps[:]), "s", 1)

            @blk.gpsimd
            def _(gpsimd, sc=sc):
                k.wait(gpsimd, "s", k.tok("s"))
                gsc = c * 4 + sc
                hc, tau = gsc // 2, (gsc % 2) * 128
                dst = io["CWF"][hc:hc + 1, :].rearrange(
                    "o (e t) -> (o e) t", t=256)[:, tau:tau + 128]
                k.inc(gpsimd.dma_start(out=dst, in_=cts[:]), "qst", 16)
            ctst[0] = k.tok("qst")
        st_tok[0] = k.tok("qst")

    @blk.gpsimd
    def _(gpsimd):
        k.wait(gpsimd, "qst", k.tok("qst"))
        k.inc(gpsimd.tensor_copy(out=crow[:, 0:1], in_=crow[:, 0:1]), "p", 1)


def build_ph5a_nc():
    nc = bass.Bass("TRN2", target_bir_lowering=False)
    io = {}
    for nm, shp, dt, kind in [
        ("XN1", [1024, TCORE], BF16, "ExternalInput"),
        ("WKV", [1024, TCORE], BF16, "ExternalInput"),
        ("R", [1024, TCORE], BF16, "ExternalInput"),
        ("sin", [128, 8], F32, "ExternalInput"),
        ("WoT", [1024, 1024], F32, "ExternalInput"),
        ("rwrT", [128, 128], F32, "ExternalInput"),
        ("tdrow", [1, 1024], F32, "ExternalInput"),
        ("iota", [1, 2048], F32, "ExternalInput"),
        ("mscol", [128, 8], F32, "ExternalInput"),
        ("id128b", [128, 128], BF16, "ExternalInput"),
        ("XMID", [1024, TCORE], F32, "ExternalOutput"),
        ("CWF", [8, 4096], BF16, "ExternalOutput"),
    ]:
        io[nm] = nc.dram_tensor(nm, shp, dt, kind=kind).ap()
    with contextlib.ExitStack() as es:
        k = K(nc, es)
        blk = es.enter_context(nc.Block())
        emit_ph5a(k, blk, io)
    return nc



def emit_ph5b(k, blk, io):
    import os
    cut5b = os.environ.get("PH5B_CUT", "full")
    nc = k.nc
    TC = 256
    NCH = TCORE // TC
    if os.environ.get("PH5B_NCH"):
        NCH = int(os.environ["PH5B_NCH"])
    es = k.es

    def sb(name, shape, dt):
        return es.enter_context(nc.sbuf_tensor(name, list(shape), dt))

    def ps(name, shape, dt=F32):
        return es.enter_context(nc.psum_tensor(name, list(shape), dt))

    wfc = [sb(f"wfc{i}", [128, 4096], BF16) for i in range(8)]
    wmlp = [sb(f"wmlp{i}", [128, 1024], BF16) for i in range(32)]
    stg = [sb(f"p5bstg{i}", [128, 1024], F32) for i in range(2)]
    xm_s = sb("xm5b", [128, 8 * TC], F32)
    xmb = sb("xmb5b", [128, 8 * TC], BF16)
    sq5 = sb("sq5b", [128, 8 * TC], BF16)
    nm2 = sb("nm5b", [128, 8 * TC], BF16)
    lr = sb("lr5b", [128, 2 * TC], F32)
    h2s = sb("h2s5b", [128, 32 * TC], BF16)
    s1o = sb("s1o5b", [128, 8 * TC], F32)
    rows5 = sb("p5brows", [1, 5 * TC], F32)
    scr5 = sb("p5bscr", [128, TC], F32)
    onesb = sb("onesb5b", [128, 1], BF16)
    onesrow = sb("onesrow5b", [1, 128], F32)
    mls = sb("mls5b", [128, 8], F32)

    fc_ps = [ps(f"fc_ps{i}", [128, 512]) for i in range(2)]
    mlp_ps = [ps(f"mlp5_ps{i}", [128, 512]) for i in range(2)]
    su_ps = ps("su5b_ps", [1, TC])
    sq_ps = ps("sq5b_ps", [1, TC])
    bc_ps = ps("bc5b_ps", [128, TC])

    @blk.sync
    def _(sync):
        k.inc(sync.dma_start(out=onesb[:], in_=io["onesb"]), "q", 16)
        k.inc(sync.dma_start(out=onesrow[:], in_=io["onesrow"]), "q", 16)
        k.inc(sync.dma_start(out=mls[:], in_=io["mlscol"]), "q", 16)
    c_tok = k.tok("q")

    ws1, wv1 = emit_weight_load(k, blk, io["WfcT"], wfc, stg, piece=1024,
                                qname="q", cname="cw")
    ws2, wv2 = emit_weight_load(k, blk, io["WmlpT"], wmlp, stg, piece=1024,
                                qname="q", cname="cw", prev=(ws1, wv1))

    st_tok = [0]
    prev_s1v = [0]
    prev_norm_vn = [0]
    prev_norm_v = [0]
    mlp_tok = [0]

    for c in range(NCH):
        @blk.sync
        def _(sync):
            if c >= 1:
                k.wait(sync, "t", k.tok("t"))
                k.wait(sync, "v", k.tok("v"))
                k.wait(sync, "vn", k.tok("vn"))
                k.wait(sync, "p", k.tok("p"))
            k.inc(sync.dma_start(
                out=xm_s.ap().rearrange("p (pb t) -> p pb t", t=TC),
                in_=io["XMID"][:, c * TC:(c + 1) * TC].rearrange(
                    "(pb p) t -> p pb t", p=128)), "q", 16)
        ld_tok = k.tok("q")

        @blk.gpsimd
        def _(gpsimd):
            k.wait(gpsimd, "q", ld_tok)
            k.wait(gpsimd, "t", k.tok("t"))
            k.inc(gpsimd.tensor_copy(out=xmb[:], in_=xm_s[:]), "p", 1)

        emit_norm(k, blk, W=TC, npb=8, onesb=onesb, onesrow=onesrow,
                  xb16=xmb, sqscr=sq5, rows=rows5, bc_ps=bc_ps,
                  su_ps=su_ps, sq_ps=sq_ps,
                  out=nm2, xf32=None, scr=scr5, eps=1e-6, with_mean=False,
                  pre=("p", k.tok("p")), tag="",
                  bc_war=prev_norm_vn[0], stat_war=prev_norm_v[0])
        nm_tok = k.tok("vn")
        prev_norm_vn[0] = nm_tok
        prev_norm_v[0] = k.tok("v")

        if cut5b == "nofc":
            @blk.vector
            def _(vector):
                k.wait(vector, "vn", nm_tok)
                if c >= 1:
                    k.wait(vector, "qst", st_tok[0])
                for m in range(8):
                    k.inc(vector.tensor_copy(
                        out=s1o[:, m * TC:(m + 1) * TC],
                        in_=xm_s[:, m * TC:(m + 1) * TC]), "v", 1)
            prev_s1v[0] = k.tok("v")

            @blk.gpsimd
            def _(gpsimd):
                k.wait(gpsimd, "v", k.tok("v"))
                k.inc(gpsimd.dma_start(
                    out=io["S1"][:, c * TC:(c + 1) * TC].rearrange(
                        "(pb p) t -> p pb t", p=128),
                    in_=s1o.ap().rearrange("p (pb t) -> p pb t", t=TC)),
                    "qst", 16)
            st_tok[0] = k.tok("qst")
            continue

        for grp in range(4):
            t6_b = k.tok("t")
            s6_b = k.tok("s")
            p6_b = k.tok("p")

            @blk.tensor
            def _(tensor):
                if c == 0 and grp == 0:
                    for wsx, wvx in [(ws1, wv1), (ws2, wv2)]:
                        k.wait(tensor, wsx[0], wvx[0])
                        k.wait(tensor, wsx[1], wvx[1])
                k.wait(tensor, "vn", nm_tok)
                if c >= 1 and grp == 0:
                    k.wait(tensor, "s", s6_b)
                for hb8 in range(8):
                    hb = grp * 8 + hb8
                    j = hb8 % 2
                    if grp > 0 or hb8 >= 2:
                        k.wait(tensor, "s", s6_b + hb8 - 1 if hb8 >= 2
                               else k.tok("s"))
                    for kb in range(8):
                        ins = tensor.matmul(
                            out=fc_ps[j][:, 0:TC],
                            lhsT=wfc[kb][:, hb * 128:(hb + 1) * 128],
                            rhs=nm2[:, kb * TC:(kb + 1) * TC],
                            start=(kb == 0), stop=(kb == 7))
                    k.inc(ins, "t", 1)

            @blk.scalar
            def _(scalar):
                for hb8 in range(8):
                    j = hb8 % 2
                    k.wait(scalar, "t", t6_b + hb8 + 1)
                    if hb8 >= 2:
                        k.wait(scalar, "p", p6_b + hb8 - 1)
                    import os as _os
                    if _os.environ.get("PH5B_RELU"):
                        k.inc(scalar.activation(out=lr[:, j * TC:(j + 1) * TC],
                                                in_=fc_ps[j][:, 0:TC],
                                                func=AF.Relu), "s", 1)
                    else:
                        k.inc(scalar.activation(out=lr[:, j * TC:(j + 1) * TC],
                                                in_=fc_ps[j][:, 0:TC],
                                                func=AF.Prelu, alpha=0.5), "s", 1)

            @blk.gpsimd
            def _(gpsimd):
                if c >= 1 and grp == 0:
                    k.wait(gpsimd, "t", mlp_tok[0])
                for hb8 in range(8):
                    hb = grp * 8 + hb8
                    j = hb8 % 2
                    k.wait(gpsimd, "s", s6_b + hb8 + 1)
                    k.inc(gpsimd.tensor_mul(
                        out=h2s[:, hb * TC:(hb + 1) * TC],
                        in0=lr[:, j * TC:(j + 1) * TC],
                        in1=lr[:, j * TC:(j + 1) * TC]), "p", 1)

        p6_all = k.tok("p")
        t6_mb = k.tok("t")

        t6_m = k.tok("t")
        v6_m = k.tok("v")

        @blk.tensor
        def _(tensor):
            k.wait(tensor, "p", p6_all)
            for m in range(8):
                j = m % 2
                if c >= 1 and m < 2:
                    k.wait(tensor, "v", prev_s1v[0])
                if m >= 2:
                    k.wait(tensor, "v", v6_m + (m - 2) + 1)
                for kb in range(32):
                    ins = tensor.matmul(
                        out=mlp_ps[j][:, 0:TC],
                        lhsT=wmlp[kb][:, m * 128:(m + 1) * 128],
                        rhs=h2s[:, kb * TC:(kb + 1) * TC],
                        start=(kb == 0), stop=(kb == 31))
                k.inc(ins, "t", 1)
        mlp_tok[0] = k.tok("t")

        @blk.vector
        def _(vector):
            if c >= 1:
                k.wait(vector, "qst", st_tok[0])
            for m in range(8):
                j = m % 2
                k.wait(vector, "t", t6_m + m + 1)
                k.inc(vector.scalar_tensor_tensor(
                    out=s1o[:, m * TC:(m + 1) * TC],
                    in0=mlp_ps[j][:, 0:TC],
                    scalar=mls[:, m:m + 1],
                    in1=xm_s[:, m * TC:(m + 1) * TC],
                    op0=ALU.mult, op1=ALU.add), "v", 1)
        prev_s1v[0] = k.tok("v")

        @blk.gpsimd
        def _(gpsimd):
            k.wait(gpsimd, "v", k.tok("v"))
            k.inc(gpsimd.dma_start(
                out=io["S1"][:, c * TC:(c + 1) * TC].rearrange(
                    "(pb p) t -> p pb t", p=128),
                in_=s1o.ap().rearrange("p (pb t) -> p pb t", t=TC)), "qst", 16)
        st_tok[0] = k.tok("qst")

    @blk.gpsimd
    def _(gpsimd):
        k.wait(gpsimd, "qst", k.tok("qst"))
        k.inc(gpsimd.tensor_copy(out=mls[:, 0:1], in_=mls[:, 0:1]), "p", 1)


def build_ph5b_nc():
    nc = bass.Bass("TRN2", target_bir_lowering=False)
    io = {}
    for nm, shp, dt, kind in [
        ("XMID", [1024, TCORE], F32, "ExternalInput"),
        ("WfcT", [1024, 4096], F32, "ExternalInput"),
        ("WmlpT", [4096, 1024], F32, "ExternalInput"),
        ("mlscol", [128, 8], F32, "ExternalInput"),
        ("onesb", [128, 1], BF16, "ExternalInput"),
        ("onesrow", [1, 128], F32, "ExternalInput"),
        ("S1", [1024, TCORE], F32, "ExternalOutput"),
    ]:
        io[nm] = nc.dram_tensor(nm, shp, dt, kind=kind).ap()
    with contextlib.ExitStack() as es:
        k = K(nc, es)
        blk = es.enter_context(nc.Block())
        emit_ph5b(k, blk, io)
    return nc



def emit_ph6(k, blk, io):
    import os
    cut6 = os.environ.get("PH6_CUT", "full")
    nc = k.nc
    TC = 256
    NHC = TCORE // TC
    es = k.es

    def sb(name, shape, dt):
        return es.enter_context(nc.sbuf_tensor(name, list(shape), dt))

    def ps(name, shape, dt=F32):
        return es.enter_context(nc.psum_tensor(name, list(shape), dt))

    xn6 = sb("xn6", [128, 8 * 2048], BF16)
    stg = [sb(f"p7stg{i}", [128, 1024], F32) for i in range(2)]
    accd = [sb(f"accd{i}", [128, 8 * TC], F32) for i in range(2)]
    sgs = sb("sgs6", [128, 8 * TC], BF16)
    shp = sb("shp6", [128, 8 * TC], BF16)
    prs = sb("prs6", [128, 8 * TC], BF16)
    cwb2 = sb("cwb2", [128, 2 * TC], BF16)
    accl = sb("accl6", [128, 8 * TC], F32)
    sumb = sb("sumb6", [128, 8 * TC], F32)
    s1c = sb("s1c6", [128, 8 * TC], F32)
    x1c = sb("x1c6", [128, 8 * TC], F32)
    g1b = sb("g1b6", [128, 8 * TC], BF16)
    outb = sb("outb6", [128, 8 * TC], F32)
    tscr = sb("tscr6", [128, TC], F32)

    gp_ps = [ps(f"g6_ps{i}", [128, 512]) for i in range(2)]
    up_ps = [ps(f"u6_ps{i}", [128, 512]) for i in range(2)]
    wd_ps = [ps(f"w6_ps{i}", [128, 512]) for i in range(2)]

    @blk.sync
    def _(sync):
        for c in range(4):
            k.inc(sync.dma_start(
                out=xn6.ap().rearrange("p (pb t) -> p pb t", t=2048)
                    [:, :, c * 512:(c + 1) * 512],
                in_=io["XN1"][:, c * 512:(c + 1) * 512].rearrange(
                    "(pb p) t -> p pb t", p=128)), "q", 16)
    xn_tok = k.tok("q")

    def xn_slice(pb, hc):
        return xn6[:, pb * 2048 + hc * TC: pb * 2048 + (hc + 1) * TC]

    shared_w3 = [None]

    with contextlib.ExitStack() as es2:
        def sb2(name, shape, dt):
            return es2.enter_context(nc.sbuf_tensor(name, list(shape), dt))
        sg = [sb2(f"sg6_{i}", [128, 1024], BF16) for i in range(8)]
        su = [sb2(f"su6_{i}", [128, 1024], BF16) for i in range(8)]
        sd = [sb2(f"sd6_{i}", [128, 1024], BF16) for i in range(8)]

        w1 = emit_weight_load(k, blk, io["Sg"], sg, stg, piece=1024,
                              qname="q", cname="cw")
        w2 = emit_weight_load(k, blk, io["Su"], su, stg, piece=1024,
                              qname="q", cname="cw", prev=w1)
        w3 = emit_weight_load(k, blk, io["Sd"], sd, stg, piece=1024,
                              qname="q", cname="cw", prev=w2)
        shared_w3[0] = w3

        accw = [0, 0]
        for hc in range(NHC):
            par = hc % 2
            t7_b = k.tok("t")
            s7_b = k.tok("s")
            v7_b = k.tok("v")

            @blk.tensor
            def _(tensor):
                if hc == 0:
                    for wsx, wvx in [w1, w2, w3]:
                        k.wait(tensor, wsx[0], wvx[0])
                        k.wait(tensor, wsx[1], wvx[1])
                    k.wait(tensor, "q", xn_tok)
                for pb in range(8):
                    j = pb % 2
                    if hc > 0 or pb >= 2:
                        k.wait(tensor, "s", s7_b + pb - 1 if pb >= 2
                               else k.tok("s"))
                    for kb in range(8):
                        ins = tensor.matmul(
                            out=gp_ps[j][:, 0:TC],
                            lhsT=sg[kb][:, pb * 128:(pb + 1) * 128],
                            rhs=xn_slice(kb, hc),
                            start=(kb == 0), stop=(kb == 7))
                    k.inc(ins, "t", 1)
                for pb in range(8):
                    j = pb % 2
                    if hc > 0 or pb >= 2:
                        k.wait(tensor, "v", v7_b + pb - 1 if pb >= 2
                               else k.tok("v"))
                    for kb in range(8):
                        ins = tensor.matmul(
                            out=up_ps[j][:, 0:TC],
                            lhsT=su[kb][:, pb * 128:(pb + 1) * 128],
                            rhs=xn_slice(kb, hc),
                            start=(kb == 0), stop=(kb == 7))
                    k.inc(ins, "t", 1)
                k.wait(tensor, "v", v7_b + 8)
                for m in range(8):
                    j = m % 2
                    if hc > 0 or m >= 2:
                        k.wait(tensor, "s", s7_b + 8 + m - 1 if m >= 2
                               else k.tok("s"))
                    for kb in range(8):
                        ins = tensor.matmul(
                            out=wd_ps[j][:, 0:TC],
                            lhsT=sd[kb][:, m * 128:(m + 1) * 128],
                            rhs=shp[:, kb * TC:(kb + 1) * TC],
                            start=(kb == 0), stop=(kb == 7))
                    k.inc(ins, "t", 1)

            @blk.scalar
            def _(scalar):
                k.wait(scalar, "v", v7_b)
                for pb in range(8):
                    j = pb % 2
                    k.wait(scalar, "t", t7_b + pb + 1)
                    k.inc(scalar.activation(out=sgs[:, pb * TC:(pb + 1) * TC],
                                            in_=gp_ps[j][:, 0:TC],
                                            func=AF.Silu), "s", 1)
                for m in range(8):
                    j = m % 2
                    k.wait(scalar, "t", t7_b + 16 + m + 1)
                    if m == 0:
                        k.wait(scalar, "qst", accw[par])
                    k.inc(scalar.copy(out=accd[par][:, m * TC:(m + 1) * TC],
                                      in_=wd_ps[j][:, 0:TC]), "s", 1)

            @blk.vector
            def _(vector):
                k.wait(vector, "t", t7_b)
                for pb in range(8):
                    j = pb % 2
                    k.wait(vector, "s", s7_b + pb + 1)
                    k.wait(vector, "t", t7_b + 8 + pb + 1)
                    k.inc(vector.tensor_mul(out=shp[:, pb * TC:(pb + 1) * TC],
                                            in0=sgs[:, pb * TC:(pb + 1) * TC],
                                            in1=up_ps[j][:, 0:TC]), "v", 1)

            @blk.gpsimd
            def _(gpsimd):
                k.wait(gpsimd, "s", k.tok("s"))
                k.inc(gpsimd.dma_start(
                    out=io["ACC"][:, hc * TC:(hc + 1) * TC].rearrange(
                        "(pb p) t -> p pb t", p=128),
                    in_=accd[par].ap().rearrange("p (pb t) -> p pb t", t=TC)),
                    "qst", 16)
            accw[par] = k.tok("qst")

    if cut6 == "shared":
        @blk.gpsimd
        def _(gpsimd):
            k.wait(gpsimd, "qst", k.tok("qst"))
            k.inc(gpsimd.dma_start(out=io["OUT"][:, :],
                                   in_=io["ACC"][:, :]), "qst", 16)
            k.wait(gpsimd, "qst", k.tok("qst"))
        return

    we_g = [[sb(f"weg{b}_{i}", [128, 512], BF16) for i in range(8)]
            for b in range(3)]
    we_u = [[sb(f"weu{b}_{i}", [128, 512], BF16) for i in range(8)]
            for b in range(3)]
    we_d = [[sb(f"wed{b}_{i}", [128, 1024], BF16) for i in range(4)]
            for b in range(3)]

    exp_rel = [0, 0, 0]
    exp_done = {}

    def stream_expert(e):
        buf = e % 3

        @blk.sync
        def _(sync):
            if exp_rel[buf]:
                k.wait(sync, "t", exp_rel[buf])
        prev = exp_done.get(e - 1) if e >= 1 else shared_w3[0]
        w_a = emit_weight_load(k, blk, io["Wg"][e * 1024:(e + 1) * 1024, :],
                               we_g[buf], stg, piece=512, qname="q",
                               cname="cw", prev=prev)
        w_b = emit_weight_load(k, blk, io["Wu"][e * 1024:(e + 1) * 1024, :],
                               we_u[buf], stg, piece=512, qname="q",
                               cname="cw", prev=w_a)
        w_c = emit_weight_load(k, blk, io["Wd"][e * 512:(e + 1) * 512, :],
                               we_d[buf], stg, piece=1024, qname="q",
                               cname="cw", prev=w_b)
        exp_done[e] = w_c

    stream_expert(0)
    stream_expert(1)

    accw = [0, 0]
    cw_war = [0]
    prs_war = [0]
    last_outv = [0]

    for g in range(8):
        e0, e1 = 2 * g, 2 * g + 1
        b0, b1 = e0 % 3, e1 % 3
        last = (g == 7)

        for hc in range(NHC):
            par = hc % 2
            t7_b = k.tok("t")
            s7_b = k.tok("s")
            v7_b = k.tok("v")

            @blk.sync
            def _(sync):
                k.wait(sync, "v", cw_war[0])
                for i, e in enumerate((e0, e1)):
                    src = io["CWF"][hc:hc + 1,
                                    e * TC:(e + 1) * TC].broadcast_to((128, TC))
                    k.inc(sync.dma_start(out=cwb2[:, i * TC:(i + 1) * TC],
                                         in_=src), "q", 16)
            cw_tok = k.tok("q")

            @blk.tensor
            def _(tensor):
                if hc == 0:
                    for wx in (exp_done[e0], exp_done[e1]):
                        k.wait(tensor, wx[0][0], wx[1][0])
                        k.wait(tensor, wx[0][1], wx[1][1])
                for i, bb in enumerate((b0, b1)):
                    for pb in range(4):
                        ii = 4 * i + pb
                        j = ii % 2
                        k.wait(tensor, "s", s7_b + ii - 1 if ii >= 2
                               else k.tok("s"))
                        for kb in range(8):
                            ins = tensor.matmul(
                                out=gp_ps[j][:, 0:TC],
                                lhsT=we_g[bb][kb][:, pb * 128:(pb + 1) * 128],
                                rhs=xn_slice(kb, hc),
                                start=(kb == 0), stop=(kb == 7))
                        k.inc(ins, "t", 1)
                for i, bb in enumerate((b0, b1)):
                    for pb in range(4):
                        ii = 4 * i + pb
                        j = ii % 2
                        k.wait(tensor, "v", v7_b + 2 * ii - 3 if ii >= 2
                               else k.tok("v"))
                        for kb in range(8):
                            ins = tensor.matmul(
                                out=up_ps[j][:, 0:TC],
                                lhsT=we_u[bb][kb][:, pb * 128:(pb + 1) * 128],
                                rhs=xn_slice(kb, hc),
                                start=(kb == 0), stop=(kb == 7))
                        k.inc(ins, "t", 1)

            @blk.scalar
            def _(scalar):
                k.wait(scalar, "v", v7_b)
                for ii in range(8):
                    j = ii % 2
                    k.wait(scalar, "t", t7_b + ii + 1)
                    k.inc(scalar.activation(
                        out=sgs[:, ii * TC:(ii + 1) * TC],
                        in_=gp_ps[j][:, 0:TC], func=AF.Silu), "s", 1)

            @blk.vector
            def _(vector):
                k.wait(vector, "q", cw_tok)
                k.wait(vector, "t", prs_war[0])
                for ii in range(8):
                    i = ii // 4
                    j = ii % 2
                    k.wait(vector, "s", s7_b + ii + 1)
                    k.wait(vector, "t", t7_b + 8 + ii + 1)
                    k.chain(vector, vector.tensor_mul(
                        out=tscr[:],
                        in0=sgs[:, ii * TC:(ii + 1) * TC],
                        in1=up_ps[j][:, 0:TC]), "vch")
                    k.inc(vector.tensor_copy(out=tscr[:, 0:1],
                                             in_=tscr[:, 0:1]), "v", 1)
                    k.inc(vector.tensor_mul(
                        out=prs[:, ii * TC:(ii + 1) * TC],
                        in0=tscr[:],
                        in1=cwb2[:, i * TC:(i + 1) * TC]), "v", 1)
            prod_done = k.tok("v")
            cw_war[0] = prod_done

            t7_c = k.tok("t")
            s7_d = k.tok("s")

            @blk.tensor
            def _(tensor):
                k.wait(tensor, "v", prod_done)
                for m in range(8):
                    j = m % 2
                    if m >= 2:
                        k.wait(tensor, "s", s7_d + m - 1)
                    for i, bb in enumerate((b0, b1)):
                        for kb in range(4):
                            ins = tensor.matmul(
                                out=wd_ps[j][:, 0:TC],
                                lhsT=we_d[bb][kb][:, m * 128:(m + 1) * 128],
                                rhs=prs[:, (4 * i + kb) * TC:(4 * i + kb + 1) * TC],
                                start=(i == 0 and kb == 0),
                                stop=(i == 1 and kb == 3))
                    k.inc(ins, "t", 1)
            prs_war[0] = k.tok("t")

            @blk.scalar
            def _(scalar):
                for m in range(8):
                    j = m % 2
                    k.wait(scalar, "t", t7_c + m + 1)
                    if m == 0:
                        k.wait(scalar, "qst", accw[par])
                    k.inc(scalar.copy(out=accd[par][:, m * TC:(m + 1) * TC],
                                      in_=wd_ps[j][:, 0:TC]), "s", 1)
            evac_done = k.tok("s")

            if not last:
                @blk.gpsimd
                def _(gpsimd):
                    k.wait(gpsimd, "s", evac_done)
                    k.inc(gpsimd.dma_start(
                        out=io["ACC"][:, hc * TC:(hc + 1) * TC].rearrange(
                            "(pb p) t -> p pb t", p=128),
                        in_=accd[par].ap().rearrange("p (pb t) -> p pb t", t=TC),
                        accum_op=ALU.add), "qst", 16)
                accw[par] = k.tok("qst")
            else:
                @blk.sync
                def _(sync):
                    k.wait(sync, "vx", last_outv[0])
                    for nm, dst in [("ACC", accl), ("S1", s1c), ("X1", x1c)]:
                        k.inc(sync.dma_start(
                            out=dst.ap().rearrange("p (pb t) -> p pb t", t=TC),
                            in_=io[nm][:, hc * TC:(hc + 1) * TC].rearrange(
                                "(pb p) t -> p pb t", p=128)), "q", 16)
                fin_tok = k.tok("q")
                vf_b = k.tok("vf")

                @blk.vector
                def _(vector):
                    k.wait(vector, "q", fin_tok)
                    k.wait(vector, "s", evac_done)
                    k.wait(vector, "sf", k.tok("sf"))
                    for m in range(8):
                        k.inc(vector.tensor_add(
                            out=sumb[:, m * TC:(m + 1) * TC],
                            in0=accl[:, m * TC:(m + 1) * TC],
                            in1=accd[par][:, m * TC:(m + 1) * TC]), "vf", 1)

                @blk.scalar
                def _(scalar):
                    for m in range(8):
                        k.wait(scalar, "vf", vf_b + m + 1)
                        k.inc(scalar.activation(
                            out=g1b[:, m * TC:(m + 1) * TC],
                            in_=sumb[:, m * TC:(m + 1) * TC],
                            func=AF.Silu), "sf", 1)
                sf_b = k.tok("sf") - 8

                @blk.vector
                def _(vector):
                    for m in range(8):
                        k.wait(vector, "sf", sf_b + m + 1)
                        k.chain(vector, vector.tensor_mul(
                            out=tscr[:], in0=s1c[:, m * TC:(m + 1) * TC],
                            in1=g1b[:, m * TC:(m + 1) * TC]), "vch")
                        k.inc(vector.tensor_add(
                            out=outb[:, m * TC:(m + 1) * TC],
                            in0=x1c[:, m * TC:(m + 1) * TC],
                            in1=tscr[:]), "vx", 1)
                last_outv[0] = k.tok("vx")

                @blk.gpsimd
                def _(gpsimd):
                    k.wait(gpsimd, "vx", k.tok("vx"))
                    k.inc(gpsimd.dma_start(
                        out=io["OUT"][:, hc * TC:(hc + 1) * TC].rearrange(
                            "(pb p) t -> p pb t", p=128),
                        in_=outb.ap().rearrange("p (pb t) -> p pb t", t=TC)),
                        "qst", 16)

        grp_end = k.tok("t")
        exp_rel[b0] = grp_end
        exp_rel[b1] = grp_end
        if 2 * g + 2 < 16:
            stream_expert(2 * g + 2)
        if 2 * g + 3 < 16:
            stream_expert(2 * g + 3)

    @blk.gpsimd
    def _(gpsimd):
        k.wait(gpsimd, "qst", k.tok("qst"))
        k.inc(gpsimd.tensor_copy(out=tscr[:, 0:1], in_=tscr[:, 0:1]), "p", 1)


def build_ph6_nc():
    nc = bass.Bass("TRN2", target_bir_lowering=False)
    io = {}
    for nm, shp, dt, kind in [
        ("XN1", [1024, TCORE], BF16, "ExternalInput"),
        ("CWF", [8, 4096], BF16, "ExternalInput"),
        ("S1", [1024, TCORE], F32, "ExternalInput"),
        ("X1", [1024, TCORE], F32, "ExternalInput"),
        ("Sg", [1024, 1024], F32, "ExternalInput"),
        ("Su", [1024, 1024], F32, "ExternalInput"),
        ("Sd", [1024, 1024], F32, "ExternalInput"),
        ("Wg", [16 * 1024, 512], F32, "ExternalInput"),
        ("Wu", [16 * 1024, 512], F32, "ExternalInput"),
        ("Wd", [16 * 512, 1024], F32, "ExternalInput"),
        ("OUT", [1024, TCORE], F32, "ExternalOutput"),
    ]:
        io[nm] = nc.dram_tensor(nm, shp, dt, kind=kind).ap()
    io["ACC"] = nc.dram_tensor("ACC", [1024, TCORE], F32, kind="Internal").ap()
    with contextlib.ExitStack() as es:
        k = K(nc, es)
        blk = es.enter_context(nc.Block())
        emit_ph6(k, blk, io)
    return nc



def _prep_shared(inputs):
    f = np.ascontiguousarray
    W = {}
    W["W_inT"] = f(inputs["L0_attn_W_in"].T)
    W["convw"] = f(inputs["L0_attn_conv_w"][:, 0, :].reshape(16, 128, 4)
                   .transpose(1, 0, 2).reshape(128, 64))
    W["convb"] = f(inputs["L0_attn_conv_b"].reshape(16, 128).T)
    W["onesb"] = np.ones((128, 1), BT)
    W["onesrow"] = np.ones((1, 128), np.float32)
    W["W_gT"] = f(inputs["L0_attn_W_g"].T)
    W["bgcol"] = f(inputs["L0_attn_b_g"].reshape(32, 128).T)
    W["fbcol"] = f(inputs["L0_attn_forget_base"].reshape(16, 128).T)
    W["W_outT"] = f(inputs["L0_attn_W_out"].T)
    W["ldT"] = f(inputs["L0_ffn_lora_down"].transpose(2, 0, 1)
                 .reshape(1024, 128).reshape(8, 128, 128)
                 .transpose(1, 0, 2).reshape(128, 1024))
    W["upS"] = f(inputs["L0_ffn_lora_up"].transpose(0, 2, 1)
                 .reshape(128, 1024))
    W["rwT"] = f(inputs["L0_ffn_router_w"].T.reshape(8, 128, 8)
                 .transpose(1, 0, 2).reshape(128, 64))
    W["ind8"] = f((np.arange(128)[None, :] // 16 ==
                   np.arange(8)[:, None]).astype(np.float32))
    W["id8"] = np.eye(8, dtype=np.float32)
    W["id1"] = np.ones((1, 1), np.float32)
    W["WkT"] = f(inputs["L1_attn_Wk"].T)
    W["WvT"] = f(inputs["L1_attn_Wv"].T)
    W["WrT"] = f(inputs["L1_attn_Wr"].T)
    W["tdcol"] = f(inputs["L1_attn_time_decay"].reshape(8, 128).T)
    W["tfcol"] = f(inputs["L1_attn_time_first"].reshape(8, 128).T)
    W["WoT"] = f(inputs["L1_attn_Wo"].T)
    W["rwrT"] = f(inputs["L1_ffn_Wr"].reshape(8, 128, 16)
                  .transpose(1, 0, 2).reshape(128, 128))
    W["tdrow"] = f(inputs["L1_attn_time_decay"].reshape(1, 1024))
    W["iota"] = np.arange(2048, dtype=np.float32).reshape(1, 2048)
    W["mscol"] = f(inputs["L1_attn_mixer_scale"].reshape(8, 128).T)
    W["id128b"] = np.eye(128, dtype=np.float32).astype(BT)
    W["WfcT"] = f(inputs["L1_attn_Wfc"].T)
    W["WmlpT"] = f(inputs["L1_attn_Wmlp"].T)
    W["mlscol"] = f(inputs["L1_attn_mlp_scale"].reshape(8, 128).T)
    W["Sg"] = f(inputs["L1_ffn_Sg"])
    W["Su"] = f(inputs["L1_ffn_Su"])
    W["Sd"] = f(inputs["L1_ffn_Sd"])
    W["Wg"] = f(inputs["L1_ffn_Wg"].reshape(16 * 1024, 512))
    W["Wu"] = f(inputs["L1_ffn_Wu"].reshape(16 * 1024, 512))
    W["Wd"] = f(inputs["L1_ffn_Wd"].reshape(16 * 512, 1024))
    return W


def _prep_cores(inputs):
    x = inputs["L0_attn_x"]
    per = []
    for core in range(NCORE):
        b, half = core // 2, core % 2
        t0 = half * TCORE
        d = {}
        d["xT"] = np.ascontiguousarray(x[b, t0:t0 + TCORE, :].T)
        if half == 1:
            d["xhaloT"] = np.ascontiguousarray(x[b, t0 - 3:t0, :].T)
        else:
            d["xhaloT"] = np.zeros((1024, 3), np.float32)
        d["xtok0"] = np.ascontiguousarray(x[b, 0, :].reshape(8, 128).T)
        sel = np.zeros((8, 1), np.float32)
        if half == 1:
            sel[core - 1, 0] = 1.0
        d["sel"] = sel
        per.append(d)
    return per



def build_ph1_nc():
    nc = bass.Bass("TRN2", target_bir_lowering=False)
    io = {}
    for nm, shp, dt, kind in [
        ("xT", [1024, TCORE], F32, "ExternalInput"),
        ("xhaloT", [1024, 3], F32, "ExternalInput"),
        ("W_inT", [1024, 4096], F32, "ExternalInput"),
        ("convw", [128, 64], F32, "ExternalInput"),
        ("convb", [128, 16], F32, "ExternalInput"),
        ("onesb", [128, 1], BF16, "ExternalInput"),
        ("onesrow", [1, 128], F32, "ExternalInput"),
        ("XN", [1024, TCORE], BF16, "ExternalOutput"),
        ("GG", [2048, TCORE], BF16, "ExternalOutput"),
        ("XC", [2048, TCORE], BF16, "ExternalOutput"),
    ]:
        io[nm] = nc.dram_tensor(nm, shp, dt, kind=kind).ap()
    with contextlib.ExitStack() as es:
        k = K(nc, es)
        blk = es.enter_context(nc.Block())
        emit_ph1(k, blk, io)
    return nc


def run_ph1(inputs):
    Wsh = _prep_shared(inputs)
    per = _prep_cores(inputs)
    nc = build_ph1_nc()
    in_maps = []
    for core in range(NCORE):
        m = {nm: Wsh[nm] for nm in ["W_inT", "convw", "convb", "onesb", "onesrow"]}
        m["xT"] = per[core]["xT"]
        m["xhaloT"] = per[core]["xhaloT"]
        in_maps.append(m)
    res = run_bass_kernel_spmd(nc, in_maps, core_ids=list(range(NCORE)))
    return res.results


def build_fused_nc():
    import os
    noag = bool(os.environ.get("FUSE_NOAG"))
    nc = bass.Bass("TRN2", target_bir_lowering=False)
    EI, EO, IN = "ExternalInput", "ExternalOutput", "Internal"
    t = {}
    specs = [
        ("HEND", [128, 128], F32, IN), ("agh_out", [1024, 128], F32, IN),
        ("SEND", [128, 128], F32, IN), ("ags_out", [1024, 128], F32, IN),
        ("xT", [1024, TCORE], F32, EI), ("xhaloT", [1024, 3], F32, EI),
        ("xtok0", [128, 8], F32, EI), ("sel", [8, 1], F32, EI),
        ("W_inT", [1024, 4096], F32, EI), ("convw", [128, 64], F32, EI),
        ("convb", [128, 16], F32, EI), ("onesb", [128, 1], BF16, EI),
        ("onesrow", [1, 128], F32, EI),
        ("W_gT", [2048, 4096], F32, EI), ("bgcol", [128, 32], F32, EI),
        ("fbcol", [128, 16], F32, EI),
        ("W_outT", [2048, 1024], F32, EI), ("ldT", [128, 1024], F32, EI),
        ("upS", [128, 1024], F32, EI), ("rwT", [128, 64], F32, EI),
        ("ind8", [8, 128], F32, EI), ("id8", [8, 8], F32, EI),
        ("id1", [1, 1], F32, EI),
        ("WkT", [1024, 1024], F32, EI), ("WvT", [1024, 1024], F32, EI),
        ("WrT", [1024, 1024], F32, EI), ("tdcol", [128, 8], F32, EI),
        ("tfcol", [128, 8], F32, EI),
        ("WoT", [1024, 1024], F32, EI), ("rwrT", [128, 128], F32, EI),
        ("tdrow", [1, 1024], F32, EI), ("iota", [1, 2048], F32, EI),
        ("mscol", [128, 8], F32, EI), ("id128b", [128, 128], BF16, EI),
        ("WfcT", [1024, 4096], F32, EI), ("WmlpT", [4096, 1024], F32, EI),
        ("mlscol", [128, 8], F32, EI),
        ("Sg", [1024, 1024], F32, EI), ("Su", [1024, 1024], F32, EI),
        ("Sd", [1024, 1024], F32, EI),
        ("Wg", [16 * 1024, 512], F32, EI), ("Wu", [16 * 1024, 512], F32, EI),
        ("Wd", [16 * 512, 1024], F32, EI),
        ("XN", [1024, TCORE], BF16, IN), ("GG", [2048, TCORE], BF16, IN),
        ("XC", [2048, TCORE], BF16, IN),
        ("H", [2048, TCORE], BF16, IN), ("P", [2048, TCORE], BF16, IN),
        ("X1", [1024, TCORE], F32, IN),
        ("XN1", [1024, TCORE], BF16, IN), ("WKV", [1024, TCORE], BF16, IN),
        ("R", [1024, TCORE], BF16, IN),
        ("XMID", [1024, TCORE], F32, IN), ("CWF", [8, 4096], BF16, IN),
        ("S1", [1024, TCORE], F32, IN), ("ACC", [1024, TCORE], F32, IN),
        ("OUT", [1024, TCORE], F32, EO),
        ("hinD", [1, 2048], F32, IN),
        ("sinD", [1, 1024], F32, IN),
    ]
    keep_n = int(os.environ.get("FUSE_NDECL", "999"))
    needed = set((os.environ.get("FUSE_NEEDED") or
                   "HEND,agh_out,SEND,ags_out,convb,XN,OUT,hinD,sinD,xT,"
                   "xhaloT,xtok0,sel").split(","))
    kept = []
    for i, (nm, shp, dt, kind) in enumerate(specs):
        if nm in needed or i < keep_n:
            kept.append((nm, shp, dt, kind))
    specs = kept
    import os as _o
    for nm, shp, dt, kind in specs:
        t[nm] = nc.dram_tensor(nm, shp, dt, kind=kind)
    for nm, shp, dt, kind in [x for x in []]:
        pass

    io = {nm: t[nm].ap() for nm, _, _, _ in specs}

    if "hinD" in t:
        io["hin"] = t["hinD"].ap().rearrange("o (p k) -> (o p) k", k=16)
        io["sin"] = t["sinD"].ap().rearrange("o (p k) -> (o p) k", k=8)

    with contextlib.ExitStack() as es_sem:
        k = K(nc, None, es_sem=es_sem)
        blk = es_sem.enter_context(nc.Block())
        bar_idx = [0]

        def barrier():
            if os.environ.get("FUSE_NOBAR"):
                bar_snap[0] = k.snapshot()
                return
            snap_waits = k.changed_since(bar_snap[0])
            bar_idx[0] += 1

            @blk.gpsimd
            def _(gpsimd):
                for n, v in snap_waits:
                    if n == "BAR":
                        continue
                    k.wait(gpsimd, n, v)
                k.inc(gpsimd.tensor_copy(out=bar_scr[:, 0:1],
                                         in_=bar_scr[:, 0:1]), "BAR", 1)

            @blk.sync
            def _(sync):
                k.wait(sync, "BAR", bar_idx[0])
                k.inc(sync.dma_start(out=bar_dram.ap(), in_=bar_scr[:, 0:1]),
                      "barq", 16)

            @blk.tensor
            def _(tensor):
                k.wait(tensor, "BAR", bar_idx[0])
                tensor.ldweights(bar_b[0:1, 0:1])

            @blk.scalar
            def _(scalar):
                k.wait(scalar, "BAR", bar_idx[0])
                scalar.copy(out=bar_scr[:, 1:2], in_=bar_scr[:, 1:2])

            @blk.vector
            def _(vector):
                k.wait(vector, "BAR", bar_idx[0])
                vector.tensor_copy(out=bar_scr[:, 2:3], in_=bar_scr[:, 2:3])
            bar_snap[0] = k.snapshot()

        bar_dram = nc.dram_tensor("bar_dram", [1, 1], F32, kind="Internal")
        bar_b = nc.alloc_sbuf_tensor("bar_b", [1, 2], BF16)
        early_cc = False

        with nc.sbuf_tensor("bar_scr", [1, 4], F32) as bar_scr:
            if early_cc:
                @blk.gpsimd
                def _(gpsimd):
                    gpsimd.wait_ge(k.s("hendq")[0], 16)
                    gpsimd.collective_compute(
                        "AllGather", ALU.bypass,
                        replica_groups=[list(range(NCORE))],
                        ins=[t["HEND"].ap().opt()],
                        outs=[t["agh_out"].ap().opt()],
                    ).then_inc(k.s("agcc0")[0])
                    k.sems["agcc0"][1] += 1
                    gpsimd.wait_ge(k.s("sendq")[0], 16)
                    gpsimd.collective_compute(
                        "AllGather", ALU.bypass,
                        replica_groups=[list(range(NCORE))],
                        ins=[t["SEND"].ap().opt()],
                        outs=[t["ags_out"].ap().opt()],
                    ).then_inc(k.s("agcc1")[0])
                    k.sems["agcc1"][1] += 1

            @blk.vector
            def _(vector):
                k.inc(vector.memset(bar_scr[:], 0.0), "BARv", 1)

            bar_snap = [k.snapshot()]

            def phase(emit, sub):
                with contextlib.ExitStack() as es_ph:
                    k.es = es_ph
                    emit(k, blk, sub)
                barrier()

            NPH = int(os.environ.get("FUSE_PHASES", "7"))
            if NPH >= 1:
                phase(emit_ph1, io)
            if NPH >= 2:
                phase(emit_ph2, io)

            agmin = bool(os.environ.get("FUSE_AGMIN"))
            with contextlib.ExitStack() as es_ag:
                k.es = es_ag
                if agmin:
                    @blk.gpsimd
                    def _(gpsimd):
                        if not noag:
                            gpsimd.collective_compute(
                                "AllGather", ALU.bypass,
                                replica_groups=[list(range(NCORE))],
                                ins=[t["HEND"].ap().opt()],
                                outs=[t["agh_out"].ap().opt()],
                            ).then_inc(k.s("agcc")[0], 16)
                            k.sems["agcc"][1] += 16
                            k.wait(gpsimd, "agcc")
                ag_s = es_ag.enter_context(nc.sbuf_tensor("agh_s", [8, 2048], F32))
                row_s = es_ag.enter_context(nc.sbuf_tensor("agh_row", [1, 2048], F32))
                sel_s = es_ag.enter_context(nc.sbuf_tensor("sel_s", [8, 1], F32))
                row_ps = es_ag.enter_context(nc.psum_tensor("agh_ps", [1, 2048], F32))

                @blk.gpsimd
                def _(gpsimd):
                    if agmin:
                        return
                    if not noag and not early_cc:
                        k.wait(gpsimd, "hendq", 16)
                        gpsimd.collective_compute(
                            "AllGather", ALU.bypass,
                            replica_groups=[list(range(NCORE))],
                            ins=[t["HEND"].ap().opt()],
                            outs=[t["agh_out"].ap().opt()],
                        ).then_inc(k.s("agcc0")[0])
                        k.sems["agcc0"][1] += 1
                    if not noag:
                        k.wait(gpsimd, "agcc0", 1)
                    else:
                        gpsimd.tensor_copy(out=ag_s[0:1, 0:1], in_=ag_s[0:1, 0:1])
                    k.inc(gpsimd.dma_start(
                        out=ag_s.ap().rearrange("r (p c) -> r p c", c=16),
                        in_=t["agh_out"].ap()[:, 0:16].rearrange(
                            "(r p) c -> r p c", p=128)),
                        "agq", 16)
                    k.inc(gpsimd.dma_start(out=sel_s[:], in_=io["sel"]), "agq", 16)

                @blk.tensor
                def _(tensor):
                    if agmin:
                        return
                    k.wait(tensor, "agq", k.tok("agq"))
                    for i in range(4):
                        k.inc(tensor.matmul(out=row_ps[:, i * 512:(i + 1) * 512],
                                            lhsT=sel_s[:],
                                            rhs=ag_s[:, i * 512:(i + 1) * 512],
                                            start=True, stop=True), "agt", 1)

                @blk.scalar
                def _(scalar):
                    if agmin:
                        return
                    k.wait(scalar, "agt", k.tok("agt"))
                    k.inc(scalar.copy(out=row_s[:], in_=row_ps[:]), "ags", 1)

                @blk.gpsimd
                def _(gpsimd):
                    if agmin:
                        return
                    k.wait(gpsimd, "ags", k.tok("ags"))
                    k.inc(gpsimd.dma_start(out=t["hinD"].ap(), in_=row_s[:]),
                          "agq", 16)
            barrier()

            if NPH >= 3:
                phase(emit_ph3, io)
            if NPH >= 4:
                phase(emit_ph4, io)

            with contextlib.ExitStack() as es_ag:
                k.es = es_ag
                ag2_s = es_ag.enter_context(nc.sbuf_tensor("ags_s", [8, 1024], F32))
                row2_s = es_ag.enter_context(nc.sbuf_tensor("ags_row", [1, 1024], F32))
                sel2_s = es_ag.enter_context(nc.sbuf_tensor("sel2_s", [8, 1], F32))
                row2_ps = es_ag.enter_context(nc.psum_tensor("ags_ps", [1, 1024], F32))

                @blk.gpsimd
                def _(gpsimd):
                    if agmin:
                        return
                    if not noag and not early_cc:
                        k.wait(gpsimd, "sendq", 16)
                        gpsimd.collective_compute(
                            "AllGather", ALU.bypass,
                            replica_groups=[list(range(NCORE))],
                            ins=[t["SEND"].ap().opt()],
                            outs=[t["ags_out"].ap().opt()],
                        ).then_inc(k.s("agcc1")[0])
                        k.sems["agcc1"][1] += 1
                    if not noag:
                        k.wait(gpsimd, "agcc1", 1)
                    else:
                        gpsimd.tensor_copy(out=ag2_s[0:1, 0:1], in_=ag2_s[0:1, 0:1])
                    k.inc(gpsimd.dma_start(
                        out=ag2_s.ap().rearrange("r (p c) -> r p c", c=8),
                        in_=t["ags_out"].ap()[:, 0:8].rearrange(
                            "(r p) c -> r p c", p=128)),
                        "agq", 16)
                    k.inc(gpsimd.dma_start(out=sel2_s[:], in_=io["sel"]), "agq", 16)

                @blk.tensor
                def _(tensor):
                    k.wait(tensor, "agq", k.tok("agq"))
                    for i in range(2):
                        k.inc(tensor.matmul(out=row2_ps[:, i * 512:(i + 1) * 512],
                                            lhsT=sel2_s[:],
                                            rhs=ag2_s[:, i * 512:(i + 1) * 512],
                                            start=True, stop=True), "agt", 1)

                @blk.scalar
                def _(scalar):
                    if agmin:
                        return
                    k.wait(scalar, "agt", k.tok("agt"))
                    k.inc(scalar.copy(out=row2_s[:], in_=row2_ps[:]), "ags", 1)

                @blk.gpsimd
                def _(gpsimd):
                    if agmin:
                        return
                    k.wait(gpsimd, "ags", k.tok("ags"))
                    k.inc(gpsimd.dma_start(out=t["sinD"].ap(), in_=row2_s[:]),
                          "agq", 16)
            barrier()

            if NPH >= 5:
                phase(emit_ph5a, io)
            if NPH >= 6:
                phase(emit_ph5b, io)
            if NPH >= 7:
                phase(emit_ph6, io)
            if NPH < 7:
                @blk.gpsimd
                def _(gpsimd):
                    k.inc(gpsimd.dma_start(out=io["OUT"][0:128, 0:16],
                                           in_=t["HEND"].ap()[:, 0:16]),
                          "qst", 16)
                    k.wait(gpsimd, "qst", k.tok("qst"))
    return nc


_NC_CACHE = {}


def _get_nc(name, builder):
    if name not in _NC_CACHE:
        _NC_CACHE[name] = builder()
    return _NC_CACHE[name]


FUSED = True


def kernel(**inputs):
    Wsh = _prep_shared(inputs)
    per = _prep_cores(inputs)
    cores = list(range(NCORE))

    if FUSED:
        nc = _get_nc("fused", build_fused_nc)
        wn = ["W_inT", "convw", "convb", "onesb", "onesrow", "W_gT", "bgcol",
              "fbcol", "W_outT", "ldT", "upS", "rwT", "ind8", "id8", "id1",
              "WkT", "WvT", "WrT", "tdcol", "tfcol", "WoT", "rwrT", "tdrow",
              "iota", "mscol", "id128b", "WfcT", "WmlpT", "mlscol",
              "Sg", "Su", "Sd", "Wg", "Wu", "Wd"]
        maps = []
        for c in cores:
            m = {n: Wsh[n] for n in wn}
            m.update(xT=per[c]["xT"], xhaloT=per[c]["xhaloT"],
                     xtok0=per[c]["xtok0"], sel=per[c]["sel"])
            maps.append(m)
        res = run_bass_kernel_spmd(nc, maps, core_ids=cores).results
        out = np.empty((4, 4096, 1024), np.float32)
        for c in cores:
            b, half = c // 2, c % 2
            t0 = half * TCORE
            out[b, t0:t0 + TCORE, :] = np.asarray(res[c]["OUT"]).T
        return out

    def run(name, builder, maps):
        nc = _get_nc(name, builder)
        return run_bass_kernel_spmd(nc, maps, core_ids=cores).results

    m1 = [dict(xT=per[c]["xT"], xhaloT=per[c]["xhaloT"],
               **{n: Wsh[n] for n in ["W_inT", "convw", "convb", "onesb",
                                      "onesrow"]}) for c in cores]
    r1 = run("ph1", build_ph1_nc, m1)

    m2 = [dict(XC=r1[c]["XC"], W_gT=Wsh["W_gT"], bgcol=Wsh["bgcol"],
               fbcol=Wsh["fbcol"]) for c in cores]
    r2 = run("ph2", build_ph2_nc, m2)

    m3 = []
    for c in cores:
        hin = (np.asarray(r2[c - 1]["HEND"])[:, 0:16] if c % 2 == 1
               else np.zeros((128, 16), np.float32))
        m3.append(dict(H=r2[c]["H"], P=r2[c]["P"], GG=r1[c]["GG"],
                       XN=r1[c]["XN"], xT=per[c]["xT"], hin=hin,
                       xtok0=per[c]["xtok0"],
                       **{n: Wsh[n] for n in ["W_outT", "ldT", "upS", "rwT",
                                              "ind8", "id8", "id1", "onesb",
                                              "onesrow"]}))
    r3 = run("ph3", build_ph3_nc, m3)

    m4 = [dict(X1=r3[c]["X1"],
               **{n: Wsh[n] for n in ["WkT", "WvT", "WrT", "tdcol", "tfcol",
                                      "onesb", "onesrow"]}) for c in cores]
    r4 = run("ph4", build_ph4_nc, m4)

    m5 = []
    for c in cores:
        sin = (np.asarray(r4[c - 1]["SEND"])[:, 0:8] if c % 2 == 1
               else np.zeros((128, 8), np.float32))
        m5.append(dict(XN1=r4[c]["XN1"], WKV=r4[c]["WKV"], R=r4[c]["R"],
                       sin=sin,
                       **{n: Wsh[n] for n in ["WoT", "rwrT", "tdrow", "iota",
                                              "mscol", "id128b"]}))
    r5 = run("ph5a", build_ph5a_nc, m5)

    m6 = [dict(XMID=r5[c]["XMID"],
               **{n: Wsh[n] for n in ["WfcT", "WmlpT", "mlscol", "onesb",
                                      "onesrow"]}) for c in cores]
    r6 = run("ph5b", build_ph5b_nc, m6)

    m7 = [dict(XN1=r4[c]["XN1"], CWF=r5[c]["CWF"], S1=r6[c]["S1"],
               X1=r3[c]["X1"],
               **{n: Wsh[n] for n in ["Sg", "Su", "Sd", "Wg", "Wu", "Wd"]})
          for c in cores]
    r7 = run("ph6", build_ph6_nc, m7)

    out = np.empty((4, 4096, 1024), np.float32)
    for c in cores:
        b, half = c // 2, c % 2
        t0 = half * TCORE
        out[b, t0:t0 + TCORE, :] = np.asarray(r7[c]["OUT"]).T
    return out
